# version 03
# baseline (speedup 1.0000x reference)
import numpy as np
import concourse.bass as bass
import concourse.mybir as mybir
from concourse.bass_utils import run_bass_kernel_spmd
from contextlib import ExitStack

F32 = mybir.dt.float32
BF16 = mybir.dt.bfloat16
I32 = mybir.dt.int32
AF = mybir.ActivationFunctionType
ALU = mybir.AluOpType
AX = mybir.AxisListType.X

D = 4096
T = 4096
B = 2
H = 32
HD = 128
INTER = 11008
NI = INTER // 128
DS = D // 128
NKS = T // 128
SCALE = 1.0 / float(np.sqrt(128.0))
N_CORES = 8
GRPS = [(3 * g, min(3 * g + 3, H)) for g in range((H + 2) // 3)]
F_BLK = 4


def _own_tiles(c):
    return [c, 15 - c, 16 + c, 31 - c]


class Ev:
    __slots__ = ("eng", "fn", "waits", "incs", "tag")

    def __init__(self, eng, fn, waits, incs, tag):
        self.eng, self.fn, self.waits, self.incs, self.tag = eng, fn, waits, incs, tag


class Prog:
    ENGS = ("y", "g", "t", "v", "s")

    def __init__(self):
        self.ev = []
        self.ms = {e: 0 for e in self.ENGS}
        self.dma = {e: 0 for e in self.ENGS}
        self._barr_n = 0

    def emit(self, eng, fn, waits=(), ms=False, dma=False, tag=""):
        incs = []
        mid = None
        if ms:
            self.ms[eng] += 1
            mid = self.ms[eng]
            incs.append(("ms", eng, 1))
        if dma:
            self.dma[eng] += 16
            mid = self.dma[eng]
            incs.append(("dma", eng, 16))
        self.ev.append(Ev(eng, fn, [w for w in waits if w is not None], incs, tag))
        return mid

    def wms(self, eng, mid):
        return ("ms", eng, mid) if mid is not None else None

    def wdma(self, eng, val):
        return ("dma", eng, val) if val is not None else None

    def barrier(self, tag="barr"):
        self._barr_n += 1
        n = self._barr_n

        def mk():
            def fn(eng, nc, sems):
                eng.drain()
                eng.sem_inc(sems["barr"], 1)
                return eng.wait_ge(sems["barr"], 5 * n)
            return fn
        for e in self.ENGS:
            self.ev.append(Ev(e, mk(), [], [], f"{tag}{n}"))



def build_program(upto=None, echo=False):
    nc = bass.Bass("TRN2", target_bir_lowering=False)
    P = Prog()

    xt = [nc.dram_tensor(f"xt{b}", [D, T], F32, kind="ExternalInput") for b in range(B)]
    xown = nc.dram_tensor("xown", [D, 1024], F32, kind="ExternalInput")
    mask01 = nc.dram_tensor("mask01", [128, NKS * 128], BF16, kind="ExternalInput")
    wqT = nc.dram_tensor("wqT", [D, D], F32, kind="ExternalInput")
    wkT = nc.dram_tensor("wkT", [D, HD], F32, kind="ExternalInput")
    wvT = nc.dram_tensor("wvT", [D, HD], F32, kind="ExternalInput")
    woT = nc.dram_tensor("woT", [D, D], F32, kind="ExternalInput")
    wqb = nc.dram_tensor("wqb", [128, H], F32, kind="ExternalInput")
    wkb = nc.dram_tensor("wkb", [128, 1], F32, kind="ExternalInput")
    wvb = nc.dram_tensor("wvb", [1, HD], F32, kind="ExternalInput")
    wob = nc.dram_tensor("wob", [128, DS], F32, kind="ExternalInput")
    wgT = nc.dram_tensor("wgT", [D, INTER], I32, kind="ExternalInput")
    wuT = nc.dram_tensor("wuT", [D, INTER], I32, kind="ExternalInput")
    wdT = nc.dram_tensor("wdT", [INTER, D], I32, kind="ExternalInput")
    sg = nc.dram_tensor("sg", [128, NI], F32, kind="ExternalInput")
    su = nc.dram_tensor("su", [128, NI], F32, kind="ExternalInput")
    sd = nc.dram_tensor("sd", [128, DS], F32, kind="ExternalInput")
    out = nc.dram_tensor("out", [D, 1024], F32, kind="ExternalOutput")
    scr_kind = "ExternalOutput" if upto else "Internal"
    x1_scr = nc.dram_tensor("x1_scr", [D, 1024], F32, kind=scr_kind)
    h_scr = nc.dram_tensor("h_scr", [INTER, 1024], BF16, kind=scr_kind)

    xt_r = [xt[b].rearrange("(dh p) t -> p dh t", p=128) for b in range(B)]
    xown_r = xown.rearrange("(dh p) t -> p dh t", p=128)
    wqT_r = wqT.rearrange("(dh p) o -> p dh o", p=128)
    wkT_r = wkT.rearrange("(dh p) o -> p dh o", p=128)
    wvT_r = wvT.rearrange("(dh p) o -> p dh o", p=128)
    woT_r = woT.rearrange("(dh p) o -> p dh o", p=128)
    wgT_r = wgT.rearrange("(dh p) i -> p dh i", p=128)
    wuT_r = wuT.rearrange("(dh p) i -> p dh i", p=128)
    wdT_r = wdT.rearrange("(ih p) o -> p ih o", p=128)
    x1_r = x1_scr.rearrange("(dh p) t -> p dh t", p=128)
    h_r = h_scr.rearrange("(ih p) t -> p ih t", p=128)
    out_r = out.rearrange("(dh p) t -> p dh t", p=128)

    ctx = ExitStack()
    e = ctx.enter_context

    def bc_mid(ap2d, reps):
        return bass.AP(ap2d.tensor, ap2d.offset, [ap2d.ap[0], [0, reps], ap2d.ap[1]])

    with ctx:
        c_ones = e(nc.sbuf_tensor("c_ones", [128, 128], BF16))
        c_eps = e(nc.sbuf_tensor("c_eps", [128, 1], F32))
        c_wqb = e(nc.sbuf_tensor("c_wqb", [128, H], F32))
        c_wkb = e(nc.sbuf_tensor("c_wkb", [128, 1], F32))
        c_wvbb = e(nc.sbuf_tensor("c_wvbb", [128, HD], BF16))
        c_wvbi = e(nc.sbuf_tensor("c_wvbi", [1, HD], BF16))
        c_wob = e(nc.sbuf_tensor("c_wob", [128, DS], F32))
        c_sg = e(nc.sbuf_tensor("c_sg", [128, NI], F32))
        c_su = e(nc.sbuf_tensor("c_su", [128, NI], F32))
        c_sd = e(nc.sbuf_tensor("c_sd", [128, DS], F32))
        c_km = e(nc.sbuf_tensor("c_km", [128, 16], F32))
        c_nb = e(nc.sbuf_tensor("c_nb", [128, 16], F32))
        c_chase = e(nc.sbuf_tensor("c_chase", [128, 64], F32))
        pp = [e(nc.psum_tensor(f"pp{i}", [128, 512], F32)) for i in range(8)]
        sems = {"barr": e(nc.semaphore("barr"))}
        for en in P.ENGS:
            sems[f"ms_{en}"] = e(nc.semaphore(f"ms_{en}"))
            sems[f"dma_{en}"] = e(nc.semaphore(f"dma_{en}"))

        def chase_g():
            return P.emit("g", lambda g, nc_, S:
                          g.dma_start(c_chase[:, 0:64], xt_r[0][:, 0, 0:64]),
                          dma=True, tag="chase.g")

        def chase_y():
            return P.emit("y", lambda y, nc_, S:
                          y.dma_start(c_chase[:, 0:64], xt_r[0][:, 0, 0:64]),
                          dma=True, tag="chase.y")

        def finish(dump_list=(), stacks=()):
            P.barrier("predump")
            for name, sb, ncols in dump_list:
                dbg = nc.dram_tensor("dbg_" + name, [128, ncols], sb.dtype,
                                     kind="ExternalOutput")
                nchunk = (ncols + 8191) // 8192
                for chv in range(nchunk):
                    lo = chv * 8192
                    hi = min(ncols, lo + 8192)
                    P.emit("y", (lambda sb=sb, dbg=dbg, lo=lo, hi=hi: lambda y, nc_, S:
                           y.dma_start(dbg[:, lo:hi], sb[:, lo:hi]))(), dma=True)
            fv = P.dma["y"]
            P.emit("y", lambda y, nc_, S: y.wait_ge(sems["dma_y"], fv), tag="dumpfinal")
            with nc.Block() as block:
                def mk_section(name):
                    def sec(eng):
                        for ev in P.ev:
                            if ev.eng != name:
                                continue
                            for (kind, en, val) in ev.waits:
                                eng.wait_ge(sems[f"{'ms' if kind == 'ms' else 'dma'}_{en}"], val)
                            ins = ev.fn(eng, nc, sems)
                            for (kind, en, v) in ev.incs:
                                assert ins is not None, f"inc on None instr: {ev.tag}"
                                ins.then_inc(sems[f"{'ms' if kind == 'ms' else 'dma'}_{en}"], v)
                    return sec
                block.sync(mk_section("y"))
                block.gpsimd(mk_section("g"))
                block.tensor(mk_section("t"))
                block.vector(mk_section("v"))
                block.scalar(mk_section("s"))
            for st_ in stacks:
                st_.close()
            return nc

        mid_ones = P.emit("g", lambda g, nc_, S: g.memset(c_ones[:], 1.0), ms=True)
        P.emit("g", lambda g, nc_, S: g.memset(c_eps[:], 1e-5), ms=True)
        ld_wqb = P.emit("y", lambda y, nc_, S: y.dma_start(c_wqb[:], wqb[:]), dma=True)
        P.emit("y", lambda y, nc_, S: y.dma_start(c_wkb[:], wkb[:]), dma=True)
        P.emit("y", lambda y, nc_, S: y.dma_start(c_wob[:], wob[:]), dma=True)
        P.emit("y", lambda y, nc_, S: y.dma_start(c_sg[:], sg[:]), dma=True)
        P.emit("y", lambda y, nc_, S: y.dma_start(c_su[:], su[:]), dma=True)
        ld_sd = P.emit("y", lambda y, nc_, S: y.dma_start(c_sd[:], sd[:]), dma=True)
        ld_vbi2 = P.emit("g", lambda g, nc_, S: g.dma_start(c_wvbi[:], wvb[:]), dma=True)

        if echo:
            e_xo = nc.dram_tensor("echo_xown", [D, 1024], F32, kind="ExternalOutput")
            e_xo_r = e_xo.rearrange("(a p) t -> p a t", p=128)
            e_buf = e(nc.sbuf_tensor("e_buf", [128, 1024], F32))
            prev = None
            for a in range(DS):
                w = [P.wdma("y", prev)] if prev is not None else []
                lid_ = P.emit("y", (lambda a=a: lambda y, nc_, S:
                              y.dma_start(e_buf[:], xown_r[:, a, :]))(),
                              waits=w, dma=True, tag=f"echo.l{a}")
                prev = P.emit("y", (lambda a=a: lambda y, nc_, S:
                              y.dma_start(e_xo_r[:, a, :], e_buf[:]))(),
                              waits=[P.wdma("y", lid_)], dma=True, tag=f"echo.s{a}")

        sAB = ExitStack()
        eAB = sAB.enter_context
        own_lnxT = eAB(nc.sbuf_tensor("own_lnxT", [128, DS * 1024], BF16))
        KT = eAB(nc.sbuf_tensor("KT", [128, B * T], BF16))
        VSB = eAB(nc.sbuf_tensor("VSB", [128, B * T], BF16))

        g1 = ExitStack()
        e1 = g1.enter_context
        xbf = [e1(nc.sbuf_tensor(f"xbf{p}", [128, D], BF16)) for p in range(4)]
        xsq = [e1(nc.sbuf_tensor(f"xsq{p}", [128, D], BF16)) for p in range(2)]
        chunk = [e1(nc.sbuf_tensor(f"chunk{p}", [128, D], BF16)) for p in range(2)]
        a_f32 = e1(nc.sbuf_tensor("a_f32", [128, 8 * 128], F32))
        a_bf = e1(nc.sbuf_tensor("a_bf", [128, 4 * 128], BF16))
        c_wkbf = e1(nc.sbuf_tensor("c_wkbf", [128, D], BF16))
        c_wvbf = e1(nc.sbuf_tensor("c_wvbf", [128, D], BF16))

        ld_wk = P.emit("g", lambda g, nc_, S: g.dma_start(
            c_wkbf[:].rearrange("p (a c) -> p a c", a=DS), wkT_r[:, :, :]), dma=True)
        ld_wv = P.emit("g", lambda g, nc_, S: g.dma_start(
            c_wvbf[:].rearrange("p (a c) -> p a c", a=DS), wvT_r[:, :, :]), dma=True)

        tiles = [("sh", b, s) for b in range(B) for s in range(NKS)] + \
                [("own", b, j) for b in range(B) for j in range(4)]
        NTL = len(tiles)
        a_ld = [None] * (NTL + 2)
        a_sq = [None] * NTL; a_st = [None] * NTL
        a_var = [None] * NTL; a_rstd = [None] * NTL; a_mr = [None] * NTL
        a_ap = [None] * NTL; a_kv = [None] * NTL; a_kd = [None] * NTL
        a_vd = [None] * NTL

        def emit_a_load(i):
            if i >= NTL + 2:
                return
            if i >= NTL:
                a_ld[i] = chase_g()
                return
            kind, b, s = tiles[i]
            p = i % 4
            if kind == "sh":
                srcap = xt_r[b][:, :, s * 128:(s + 1) * 128]
            else:
                srcap = xown_r[:, :, (b * 4 + s) * 128:(b * 4 + s + 1) * 128]
            w = []
            if i >= 4:
                w = [P.wms("s", a_sq[i - 4]), P.wms("t", a_st[i - 4]),
                     P.wms("v", a_ap[i - 4])]
            a_ld[i] = P.emit("g", (lambda srcap=srcap, p=p: lambda g, nc_, S:
                             g.dma_start(xbf[p][:].rearrange("p (a c) -> p a c", a=DS),
                                         srcap))(),
                             waits=w, dma=True, tag=f"A.ld{i}")

        emit_a_load(0)
        emit_a_load(1)
        emit_a_load(2)

        m_vb = P.emit("t", lambda t, nc_, S: t.matmul(pp[7][:, 0:HD], c_ones[0:1, :],
                                                      c_wvbi[:], start=True, stop=True),
                      waits=[P.wdma("g", ld_vbi2), P.wms("g", mid_ones),
                             P.wdma("g", a_ld[0])], ms=True)
        m_vbc = P.emit("v", lambda v, nc_, S: v.tensor_copy(c_wvbb[:], pp[7][:, 0:HD]),
                       waits=[P.wms("t", m_vb)], ms=True)

        for i, (kind, b, s) in enumerate(tiles):
            emit_a_load(i + 3)
            p = i % 4
            psq = i % 2
            slack = P.wdma("g", a_ld[min(i + 2, NTL + 1)])

            w = [slack]
            if i >= 2:
                w.append(P.wms("t", a_st[i - 2]))
            a_sq[i] = P.emit("s", (lambda p=p, psq=psq: lambda s_, nc_, S:
                             s_.activation(xsq[psq][:], xbf[p][:], AF.Square))(),
                             waits=w, ms=True, tag=f"A.sq{i}")

            def mk_stats(p=p, psq=psq):
                def fn(t, nc_, S):
                    last = None
                    for dh in range(DS):
                        t.matmul(pp[psq][:, 0:128], c_ones[:],
                                 xbf[p][:, dh * 128:(dh + 1) * 128],
                                 start=(dh == 0), stop=(dh == DS - 1))
                        last = t.matmul(pp[2 + psq][:, 0:128], c_ones[:],
                                        xsq[psq][:, dh * 128:(dh + 1) * 128],
                                        start=(dh == 0), stop=(dh == DS - 1))
                    return last
                return fn
            w = [slack, P.wms("s", a_sq[i]), P.wms("g", mid_ones)]
            if i >= 2:
                w.append(P.wms("v", a_var[i - 2]))
            a_st[i] = P.emit("t", mk_stats(), waits=w, ms=True, tag=f"A.st{i}")

            mo, eo, vo, to = psq * 512, psq * 512 + 128, psq * 512 + 256, psq * 512 + 384
            ro, mro = psq * 256, psq * 256 + 128

            def mk_me(psq=psq, mo=mo, eo=eo):
                def fn(v, nc_, S):
                    v.tensor_scalar_mul(a_f32[:, mo:mo + 128], pp[psq][:, 0:128], 1.0 / D)
                    return v.tensor_scalar_mul(a_f32[:, eo:eo + 128],
                                               pp[2 + psq][:, 0:128], 1.0 / D)
                return fn
            w = [P.wms("t", a_st[i])]
            if i >= 2:
                w.append(P.wms("v", a_mr[i - 2]))
            me_id = P.emit("v", mk_me(), waits=w, ms=True, tag=f"A.me{i}")

            def mk_msq(mo=mo, to=to):
                def fn(v, nc_, S):
                    return v.tensor_tensor(a_f32[:, to:to + 128], a_f32[:, mo:mo + 128],
                                           a_f32[:, mo:mo + 128], ALU.mult)
                return fn
            msq_id = P.emit("v", mk_msq(), waits=[P.wms("v", me_id)], ms=True,
                            tag=f"A.msq{i}")

            def mk_var(eo=eo, vo=vo, to=to):
                def fn(v, nc_, S):
                    return v.tensor_tensor(a_f32[:, vo:vo + 128], a_f32[:, eo:eo + 128],
                                           a_f32[:, to:to + 128], ALU.subtract)
                return fn
            a_var[i] = P.emit("v", mk_var(), waits=[P.wms("v", msq_id)], ms=True,
                              tag=f"A.var{i}")

            def mk_ln(vo=vo, to=to):
                def fn(s_, nc_, S):
                    return s_.activation(a_f32[:, to:to + 128], a_f32[:, vo:vo + 128],
                                         AF.Ln, bias=c_eps[:, 0:1])
                return fn
            ln_id = P.emit("s", mk_ln(), waits=[P.wms("v", a_var[i])], ms=True,
                           tag=f"A.ln{i}")

            def mk_rstd(to=to, ro=ro):
                def fn(s_, nc_, S):
                    return s_.activation(a_bf[:, ro:ro + 128], a_f32[:, to:to + 128],
                                         AF.Exp, scale=-0.5)
                return fn
            a_rstd[i] = P.emit("s", mk_rstd(), waits=[P.wms("s", ln_id)], ms=True,
                               tag=f"A.rstd{i}")

            def mk_mr(mo=mo, ro=ro, mro=mro):
                def fn(v, nc_, S):
                    return v.tensor_tensor(a_bf[:, mro:mro + 128], a_f32[:, mo:mo + 128],
                                           a_bf[:, ro:ro + 128], ALU.mult)
                return fn
            a_mr[i] = P.emit("v", mk_mr(), waits=[P.wms("s", a_rstd[i])], ms=True,
                             tag=f"A.mr{i}")

            if kind == "own":
                dst3 = own_lnxT[:].rearrange("p (a c) -> p a c", c=1024)[
                    :, :, (b * 4 + s) * 128:(b * 4 + s + 1) * 128]
            else:
                dst3 = chunk[psq][:].rearrange("p (a c) -> p a c", a=DS)

            def mk_apply(p=p, dst3=dst3, ro=ro, mro=mro):
                def fn(v, nc_, S):
                    rb = bc_mid(a_bf[:, ro:ro + 128], DS)
                    mb = bc_mid(a_bf[:, mro:mro + 128], DS)
                    x3 = xbf[p][:].rearrange("p (a c) -> p a c", a=DS)
                    v.tensor_tensor(dst3, x3, rb, ALU.mult)
                    return v.tensor_tensor(dst3, dst3, mb, ALU.subtract)
                return fn
            w = [P.wms("v", a_mr[i]), slack]
            if kind == "sh" and i >= 2 and tiles[i - 2][0] == "sh":
                w.append(P.wms("t", a_kv[i - 2]))
            a_ap[i] = P.emit("v", mk_apply(), waits=w, ms=True, tag=f"A.ap{i}")

            if kind == "sh":
                def mk_kv(psq=psq):
                    def fn(t, nc_, S):
                        last = None
                        for dh in range(DS):
                            t.matmul(pp[4 + psq][:, 0:128],
                                     c_wkbf[:, dh * 128:(dh + 1) * 128],
                                     chunk[psq][:, dh * 128:(dh + 1) * 128],
                                     start=(dh == 0), stop=(dh == DS - 1))
                            last = t.matmul(pp[6 + psq][:, 0:128],
                                            chunk[psq][:, dh * 128:(dh + 1) * 128],
                                            c_wvbf[:, dh * 128:(dh + 1) * 128],
                                            start=(dh == 0), stop=(dh == DS - 1))
                        return last
                    return fn
                w = [P.wms("v", a_ap[i]), P.wdma("g", ld_wv)]
                if i >= 2:
                    w += [P.wms("s", a_kd[i - 2]), P.wms("v", a_vd[i - 2])]
                if i <= 1:
                    w.append(P.wms("v", m_vbc))
                a_kv[i] = P.emit("t", mk_kv(), waits=w, ms=True, tag=f"A.kv{i}")
                koff = (b * NKS + s) * 128

                def mk_kd(psq=psq, koff=koff):
                    def fn(s_, nc_, S):
                        return s_.activation(KT[:, koff:koff + 128], pp[4 + psq][:, 0:128],
                                             AF.Identity, bias=c_wkb[:, 0:1])
                    return fn
                a_kd[i] = P.emit("s", mk_kd(), waits=[P.wms("t", a_kv[i])], ms=True,
                                 tag=f"A.kd{i}")

                def mk_vd(psq=psq, koff=koff):
                    def fn(v, nc_, S):
                        return v.tensor_tensor(VSB[:, koff:koff + 128],
                                               pp[6 + psq][:, 0:128],
                                               c_wvbb[:], ALU.add)
                    return fn
                a_vd[i] = P.emit("v", mk_vd(), waits=[P.wms("t", a_kv[i]),
                                                      P.wms("v", m_vbc)],
                                 ms=True, tag=f"A.vd{i}")

        km_id = [None] * B
        prev_mm = None
        for b in range(B):
            w = [P.wms("s", a_kd[b * NKS + NKS - 1]), P.wms("t", a_st[NTL - 1])]
            if prev_mm is not None:
                w.append(P.wms("t", prev_mm))
            ksq_id = P.emit("s", (lambda b=b: lambda s_, nc_, S:
                            s_.activation(xsq[0][:], KT[:, b * T:(b + 1) * T], AF.Square))(),
                            waits=w, ms=True, tag=f"K.sq{b}")
            red_ids = []
            for ch in range(8):
                w = [P.wms("s", ksq_id)]
                if ch == 0:
                    w.append(P.wms("v", a_vd[b * NKS + NKS - 1]))
                    w.append(P.wms("v", a_var[NTL - 1]))
                else:
                    w.append(P.wms("v", red_ids[-1]))
                mmid = P.emit("t", (lambda ch=ch: lambda t, nc_, S:
                              t.matmul(pp[0][:], c_ones[:],
                                       xsq[0][:, ch * 512:(ch + 1) * 512],
                                       start=True, stop=True))(),
                              waits=w, ms=True, tag=f"K.mm{b}.{ch}")
                red_ids.append(P.emit("v", (lambda ch=ch: lambda v, nc_, S:
                               v.reduce_max(c_km[:, 8 + ch:9 + ch], pp[0][:], axis=AX))(),
                               waits=[P.wms("t", mmid)], ms=True, tag=f"K.rd{b}.{ch}"))
                prev_mm = mmid
            km_id[b] = P.emit("v", (lambda b=b: lambda v, nc_, S:
                              v.reduce_max(c_km[:, b:b + 1], c_km[:, 8:16], axis=AX))(),
                              waits=[P.wms("v", red_ids[-1])], ms=True, tag=f"K.mx{b}")

        if upto == "A":
            return finish([("own_lnxT", own_lnxT, DS * 1024), ("KT", KT, B * T),
                           ("VSB", VSB, B * T), ("km", c_km, 16)], stacks=[g1, sAB])
        P.barrier("postA")
        g1.close()

        gBC = ExitStack()
        e2 = gBC.enter_context
        wqh = [e2(nc.sbuf_tensor(f"wqh{r}", [128, 2048], BF16)) for r in range(4)]
        QT = e2(nc.sbuf_tensor("QT", [128, 4 * 1024], BF16))
        expb = [e2(nc.sbuf_tensor(f"expb{r}", [128, 512], BF16)) for r in range(8)]
        recb = e2(nc.sbuf_tensor("recb", [128, 512], F32))
        c_mask = e2(nc.sbuf_tensor("c_mask", [128, NKS * 128], BF16))
        sCD = ExitStack()
        attnT = sCD.enter_context(nc.sbuf_tensor("attnT", [128, H * 1024], BF16,
                                                 side="right"))

        ld_msk = P.emit("y", lambda y, nc_, S: y.dma_start(c_mask[:], mask01[:]), dma=True)

        wq_ld = [None] * (2 * H + 2)
        qproj = {}
        qdrain = {}
        qsm_last = None
        scdone = {}
        cdrain = {}
        sc_list = []
        exp_list = []
        msk_list = []
        av_list = []

        def emit_wq_load(idx):
            if idx >= 2 * H + 2:
                return
            if idx >= 2 * H:
                wq_ld[idx] = chase_g()
                return
            h, half = divmod(idx, 2)
            ring = idx % 4
            w = []
            if idx >= 4:
                w.append(P.wms("t", qproj[(idx - 4) // 2]))
            srcap = wqT_r[:, 16 * half:16 * (half + 1), h * 128:(h + 1) * 128]
            wq_ld[idx] = P.emit(
                "g", (lambda ring=ring, srcap=srcap: lambda g, nc_, S:
                      g.dma_start(wqh[ring][:].rearrange("p (a c) -> p a c", a=16),
                                  srcap))(),
                waits=w, dma=True, tag=f"B.wq{h}.{half}")

        emit_wq_load(0)
        emit_wq_load(1)

        GRPS2 = [(2 * g, 2 * g + 2) for g in range(H // 2)]

        def emit_av(n, gi, b):
            s, hi = divmod(n, 2)
            jf = s // 8
            nq = (4 - jf) * 128
            qoff = jf * 128
            er = (gbase(gi, b) + n) % 8

            def mk_av(b=b, s=s, hi=hi, er=er, nq=nq, qoff=qoff):
                def fn(t, nc_, S):
                    t.matmul(pp[hi][:, qoff:qoff + nq],
                             VSB[:, (b * NKS + s) * 128:(b * NKS + s + 1) * 128],
                             expb[er][:, 0:nq],
                             start=(s == 0), stop=(s == NKS - 1),
                             skip_group_check=True)
                    return t.matmul(pp[2 + hi][:, qoff:qoff + nq],
                                    c_ones[:], expb[er][:, 0:nq],
                                    start=(s == 0), stop=(s == NKS - 1),
                                    skip_group_check=True)
                return fn
            w = [P.wms("v", msk_list[gbase(gi, b) + n])]
            if s == 0:
                pk = (gi, b - 1) if b == 1 else ((gi - 1, B - 1) if gi > 0 else None)
                if pk in cdrain:
                    w.append(P.wms("v", cdrain[pk]))
            av_id = P.emit("t", mk_av(), waits=w, ms=True, tag=f"C.av{gi}.{b}.{s}.{hi}")
            av_list.append(av_id)
            return av_id

        sweep_base = {}

        def gbase(gi, b):
            return sweep_base[(gi, b)]

        for gi, (h0, h1) in enumerate(GRPS2):
            nh = h1 - h0
            for h in range(h0, h1):
                emit_wq_load(2 * h + 2)
                emit_wq_load(2 * h + 3)
                pb = 4 + (h % 2) * 2

                def mk_qmm(h=h, pb=pb):
                    def fn(t, nc_, S):
                        last = None
                        for half in range(2):
                            ring = (h * 2 + half) % 4
                            for dhh in range(16):
                                dh = half * 16 + dhh
                                lhsT = wqh[ring][:, dhh * 128:(dhh + 1) * 128]
                                t.matmul(pp[pb][:], lhsT,
                                         own_lnxT[:, dh * 1024:dh * 1024 + 512],
                                         start=(dh == 0), stop=(dh == DS - 1))
                                last = t.matmul(pp[pb + 1][:], lhsT,
                                                own_lnxT[:, dh * 1024 + 512:(dh + 1) * 1024],
                                                start=(dh == 0), stop=(dh == DS - 1))
                        return last
                    return fn
                w = [P.wdma("g", wq_ld[min(2 * h + 3, 2 * H + 1)])]
                if h >= 2:
                    w.append(P.wms("v", qdrain[h - 2]))
                if gi > 0:
                    w.append(P.wms("v", cdrain[gi - 1]))
                    w.append(P.wms("s", exp_list[-1]))
                qproj[h] = P.emit("t", mk_qmm(), waits=w, ms=True, tag=f"B.qmm{h}")

                slot = h % 4

                def mk_qdr(h=h, pb=pb, slot=slot):
                    def fn(v, nc_, S):
                        v.tensor_scalar_add(QT[:, slot * 1024:slot * 1024 + 512],
                                            pp[pb][:], c_wqb[:, h:h + 1])
                        return v.tensor_scalar_add(QT[:, slot * 1024 + 512:(slot + 1) * 1024],
                                                   pp[pb + 1][:], c_wqb[:, h:h + 1])
                    return fn
                w = [P.wms("t", qproj[h]), P.wdma("y", ld_wqb)]
                if gi >= 1:
                    w.append(P.wms("t", scdone[gi - 1]))
                qdrain[h] = P.emit("v", mk_qdr(), waits=w, ms=True, tag=f"B.qdr{h}")

            last_red = None
            for hi, h in enumerate(range(h0, h1)):
                slot = h % 4

                def mk_qsq(slot=slot):
                    def fn(v, nc_, S):
                        v.tensor_tensor(expb[0][:], QT[:, slot * 1024:slot * 1024 + 512],
                                        QT[:, slot * 1024:slot * 1024 + 512], ALU.mult)
                        return v.tensor_tensor(expb[1][:],
                                               QT[:, slot * 1024 + 512:(slot + 1) * 1024],
                                               QT[:, slot * 1024 + 512:(slot + 1) * 1024],
                                               ALU.mult)
                    return fn
                w = [P.wms("v", qdrain[h])]
                if qsm_last is not None:
                    w.append(P.wms("t", qsm_last))
                if av_list:
                    w.append(P.wms("t", av_list[-1]))
                qsq_id = P.emit("v", mk_qsq(), waits=w, ms=True, tag=f"B.qsq{h}")

                def mk_qsm():
                    def fn(t, nc_, S):
                        t.matmul(pp[4][:], c_ones[:], expb[0][:], start=True, stop=True)
                        return t.matmul(pp[5][:], c_ones[:], expb[1][:],
                                        start=True, stop=True)
                    return fn
                w = [P.wms("v", qsq_id)]
                if gi > 0:
                    w.append(P.wms("v", cdrain[gi - 1]))
                if last_red is not None:
                    w.append(P.wms("v", last_red))
                qsm_last = P.emit("t", mk_qsm(), waits=w, ms=True, tag=f"B.qsm{h}")

                def mk_qrd(hi=hi):
                    def fn(v, nc_, S):
                        v.reduce_max(c_nb[:, 10 + 2 * hi:11 + 2 * hi], pp[4][:], axis=AX)
                        return v.reduce_max(c_nb[:, 11 + 2 * hi:12 + 2 * hi], pp[5][:],
                                            axis=AX)
                    return fn
                last_red = P.emit("v", mk_qrd(), waits=[P.wms("t", qsm_last)], ms=True,
                                  tag=f"B.qrd{h}")
            qmax_id = P.emit("v", (lambda nh=nh: lambda v, nc_, S:
                             v.reduce_max(c_nb[:, 0:1], c_nb[:, 10:10 + 2 * nh], axis=AX))(),
                             waits=[P.wms("v", last_red)], ms=True, tag=f"B.qmx{gi}")

            bc_ids = {}
            for b in range(B):
                base = 1 + b * 3
                pid = P.emit("v", (lambda b=b, base=base: lambda v, nc_, S:
                             v.tensor_tensor(c_nb[:, base:base + 1], c_nb[:, 0:1],
                                             c_km[:, b:b + 1], ALU.mult))(),
                             waits=[P.wms("v", qmax_id), P.wms("v", km_id[b])],
                             ms=True, tag=f"C.p{gi}.{b}")
                lid = P.emit("s", (lambda base=base: lambda s_, nc_, S:
                             s_.activation(c_nb[:, base + 1:base + 2],
                                           c_nb[:, base:base + 1],
                                           AF.Ln, bias=c_eps[:, 0:1]))(),
                             waits=[P.wms("v", pid)], ms=True, tag=f"C.l{gi}.{b}")
                eid = P.emit("s", (lambda base=base: lambda s_, nc_, S:
                             s_.activation(c_nb[:, base + 2:base + 3],
                                           c_nb[:, base + 1:base + 2],
                                           AF.Exp, scale=0.5))(),
                             waits=[P.wms("s", lid)], ms=True, tag=f"C.e{gi}.{b}")
                bc_ids[b] = P.emit("v", (lambda base=base, b=b: lambda v, nc_, S:
                                   v.tensor_scalar_mul(c_nb[:, 8 + b:9 + b],
                                                       c_nb[:, base + 2:base + 3],
                                                       -SCALE * 1.02))(),
                                   waits=[P.wms("s", eid)], ms=True, tag=f"C.bc{gi}.{b}")

            for b in range(B):
                sweep_base[(gi, b)] = len(sc_list)
                nb_steps = NKS * nh
                last_sc = None
                AV_DELAY = 3 * nh

                for s in range(NKS):
                    jf = s // 8
                    nq = (4 - jf) * 128
                    qoff = jf * 128
                    for hi, h in enumerate(range(h0, h1)):
                        slot = h % 4
                        n_loc = s * nh + hi
                        n = len(sc_list)
                        sb = 4 + (n % 4)
                        er = n % 8

                        def mk_sc(b=b, s=s, nq=nq, qoff=qoff, slot=slot, sb=sb):
                            def fn(t, nc_, S):
                                return t.matmul(
                                    pp[sb][:, 0:nq],
                                    KT[:, (b * NKS + s) * 128:(b * NKS + s + 1) * 128],
                                    QT[:, slot * 1024 + b * 512 + qoff:
                                       slot * 1024 + b * 512 + qoff + nq],
                                    start=True, stop=True, skip_group_check=True)
                            return fn
                        w = [P.wms("v", qdrain[h]), P.wms("v", bc_ids[b])]
                        if n >= 4:
                            w.append(P.wms("s", exp_list[n - 4]))
                        sc_id = P.emit("t", mk_sc(), waits=w, ms=True,
                                       tag=f"C.sc{gi}.{b}.{s}.{h}")
                        sc_list.append(sc_id)
                        last_sc = sc_id

                        def mk_exp(sb=sb, er=er, b=b, nq=nq):
                            def fn(s_, nc_, S):
                                return s_.activation(expb[er][:, 0:nq], pp[sb][:, 0:nq],
                                                     AF.Exp, bias=c_nb[:, 8 + b:9 + b],
                                                     scale=SCALE)
                            return fn
                        w = [P.wms("t", sc_id)]
                        if n >= 8:
                            w.append(P.wms("t", av_list[n - 8]))
                        exp_list.append(P.emit("s", mk_exp(), waits=w, ms=True,
                                               tag=f"C.exp{gi}.{b}.{s}.{h}"))

                        def mk_msk(er=er, s=s):
                            def fn(v, nc_, S):
                                return v.tensor_tensor(expb[er][:, 0:128],
                                                       expb[er][:, 0:128],
                                                       c_mask[:, s * 128:(s + 1) * 128],
                                                       ALU.mult)
                            return fn
                        msk_list.append(P.emit("v", mk_msk(),
                                        waits=[P.wms("s", exp_list[-1]),
                                               P.wdma("y", ld_msk)],
                                        ms=True, tag=f"C.mk{gi}.{b}.{s}.{h}"))

                        if n_loc >= AV_DELAY:
                            emit_av(n_loc - AV_DELAY, gi, b)
                for n_loc in range(nb_steps - AV_DELAY, nb_steps):
                    emit_av(n_loc, gi, b)

                dr_last = None
                for hi, h in enumerate(range(h0, h1)):
                    rec_id = P.emit("v", (lambda hi=hi: lambda v, nc_, S:
                                    v.reciprocal(recb[:], pp[2 + hi][:]))(),
                                    waits=[P.wms("t", av_list[-1])], ms=True,
                                    tag=f"C.rc{gi}.{b}.{h}")
                    aoff = h * 1024 + b * 512
                    dr_last = P.emit("v", (lambda hi=hi, aoff=aoff: lambda v, nc_, S:
                                     v.tensor_tensor(attnT[:, aoff:aoff + 512], pp[hi][:],
                                                     recb[:], ALU.mult))(),
                                     waits=[P.wms("v", rec_id)], ms=True,
                                     tag=f"C.dr{gi}.{b}.{h}")
                cdrain[(gi, b)] = dr_last
            scdone[gi] = last_sc
            cdrain[gi] = cdrain[(gi, B - 1)]

        if upto == "C":
            return finish([("attnT", attnT, H * 1024), ("QT", QT, 4 * 1024),
                           ("km", c_km, 16), ("nb", c_nb, 16)],
                          stacks=[gBC, sAB, sCD])
        P.barrier("postC")
        gBC.close()
        sAB.close()

        g3 = ExitStack()
        e3 = g3.enter_context
        wof = [e3(nc.sbuf_tensor(f"wof{r}", [128, D], BF16)) for r in range(4)]
        xow = [e3(nc.sbuf_tensor(f"xow{r}", [128, 1024], F32)) for r in range(4)]
        x1p = [e3(nc.sbuf_tensor(f"x1p{r}", [128, 1024], F32)) for r in range(4)]

        d_wo = [None] * (DS + 2); d_xo = [None] * (DS + 2)
        d_mm = [None] * DS; d_dr = [None] * DS; d_wr = [None] * DS

        def emit_d_wo(dt):
            if dt >= DS + 2:
                return
            if dt >= DS:
                d_wo[dt] = chase_g()
                return
            w = [P.wms("t", d_mm[dt - 4])] if dt >= 4 else []
            d_wo[dt] = P.emit("g", (lambda dt=dt: lambda g, nc_, S:
                              g.dma_start(wof[dt % 4][:].rearrange("p (a c) -> p a c", a=DS),
                                          woT_r[:, :, dt * 128:(dt + 1) * 128]))(),
                              waits=w, dma=True, tag=f"D.wo{dt}")

        def emit_d_xo(dt):
            if dt >= DS + 2:
                return
            if dt >= DS:
                d_xo[dt] = chase_y()
                return
            w = [P.wms("v", d_dr[dt - 4])] if dt >= 4 else []
            d_xo[dt] = P.emit("y", (lambda dt=dt: lambda y, nc_, S:
                              y.dma_start(xow[dt % 4][:], xown_r[:, dt, :]))(),
                              waits=w, dma=True, tag=f"D.xo{dt}")

        for dt in range(3):
            emit_d_wo(dt)
            emit_d_xo(dt)

        for dt in range(DS):
            emit_d_wo(dt + 3)
            emit_d_xo(dt + 3)
            pb = (dt % 3) * 2

            def mk_dmm(dt=dt, pb=pb):
                def fn(t, nc_, S):
                    last = None
                    for h in range(H):
                        lhsT = wof[dt % 4][:, h * 128:(h + 1) * 128]
                        t.matmul(pp[pb][:], lhsT, attnT[:, h * 1024:h * 1024 + 512],
                                 start=(h == 0), stop=(h == H - 1))
                        last = t.matmul(pp[pb + 1][:], lhsT,
                                        attnT[:, h * 1024 + 512:(h + 1) * 1024],
                                        start=(h == 0), stop=(h == H - 1))
                    return last
                return fn
            w = [P.wdma("g", d_wo[min(dt + 2, DS + 1)])]
            if dt >= 3:
                w.append(P.wms("v", d_dr[dt - 3]))
            d_mm[dt] = P.emit("t", mk_dmm(), waits=w, ms=True, tag=f"D.mm{dt}")

            def mk_ddr(dt=dt, pb=pb):
                def fn(v, nc_, S):
                    v.scalar_tensor_tensor(x1p[dt % 4][:, 0:512], pp[pb][:],
                                           c_wob[:, dt:dt + 1],
                                           xow[dt % 4][:, 0:512], ALU.add, ALU.add)
                    return v.scalar_tensor_tensor(x1p[dt % 4][:, 512:1024], pp[pb + 1][:],
                                                  c_wob[:, dt:dt + 1],
                                                  xow[dt % 4][:, 512:1024],
                                                  ALU.add, ALU.add)
                return fn
            w = [P.wms("t", d_mm[dt]), P.wdma("y", d_xo[min(dt + 2, DS + 1)])]
            if dt >= 2:
                w.append(P.wdma("y", d_wr[dt - 2]))
            d_dr[dt] = P.emit("v", mk_ddr(), waits=w, ms=True, tag=f"D.dr{dt}")
            d_wr[dt] = P.emit("y", (lambda dt=dt: lambda y, nc_, S:
                              y.dma_start(x1_r[:, dt, :], x1p[dt % 4][:]))(),
                              waits=[P.wms("v", d_dr[dt])], dma=True, tag=f"D.wr{dt}")

        if upto == "D":
            return finish(stacks=[g3, sCD])
        chase_y()
        chase_y()
        flush_y = P.dma["y"]
        P.emit("y", lambda y, nc_, S: y.wait_ge(sems["dma_y"], flush_y), tag="D.flush")
        P.barrier("postD")
        g3.close()
        sCD.close()

        sE = ExitStack()
        lnx2T = sE.enter_context(nc.sbuf_tensor("lnx2T", [128, DS * 1024], BF16,
                                                side="right"))
        g4 = ExitStack()
        e4 = g4.enter_context
        x1bf = e4(nc.sbuf_tensor("x1bf", [128, DS * 1024], BF16))
        x1sq = [e4(nc.sbuf_tensor(f"x1sq{r}", [128, 1024], BF16)) for r in range(2)]
        ln2f = e4(nc.sbuf_tensor("ln2f", [128, 3 * 1024], F32))
        ln2b = e4(nc.sbuf_tensor("ln2b", [128, 2 * 1024], BF16))

        e_ld = [None] * (DS + 2)
        for k in range(DS):
            e_ld[k] = P.emit("g", (lambda k=k: lambda g, nc_, S:
                             g.dma_start(x1bf[:, k * 1024:(k + 1) * 1024], x1_r[:, k, :]))(),
                             dma=True, tag=f"E.ld{k}")
        e_ld[DS] = chase_g()
        e_ld[DS + 1] = chase_g()

        e_sq = [None] * DS; e_stat = [None] * DS
        for k in range(DS):
            p = k % 2
            slack = P.wdma("g", e_ld[min(k + 2, DS + 1)])
            w = [slack]
            if k >= 2:
                w.append(P.wms("t", e_stat[k - 2]))
            e_sq[k] = P.emit("s", (lambda k=k, p=p: lambda s_, nc_, S:
                             s_.activation(x1sq[p][:], x1bf[:, k * 1024:(k + 1) * 1024],
                                           AF.Square))(),
                             waits=w, ms=True, tag=f"E.sq{k}")

            def mk_est(k=k, p=p):
                def fn(t, nc_, S):
                    t.matmul(pp[0][:], c_ones[:], x1bf[:, k * 1024:k * 1024 + 512],
                             start=(k == 0), stop=(k == DS - 1))
                    t.matmul(pp[1][:], c_ones[:], x1bf[:, k * 1024 + 512:(k + 1) * 1024],
                             start=(k == 0), stop=(k == DS - 1))
                    t.matmul(pp[2][:], c_ones[:], x1sq[p][:, 0:512],
                             start=(k == 0), stop=(k == DS - 1))
                    return t.matmul(pp[3][:], c_ones[:], x1sq[p][:, 512:1024],
                                    start=(k == 0), stop=(k == DS - 1))
                return fn
            e_stat[k] = P.emit("t", mk_est(), waits=[slack, P.wms("s", e_sq[k])],
                               ms=True, tag=f"E.st{k}")

        def mk_ef1():
            def fn(v, nc_, S):
                v.tensor_scalar_mul(ln2f[:, 0:512], pp[0][:], 1.0 / D)
                v.tensor_scalar_mul(ln2f[:, 512:1024], pp[1][:], 1.0 / D)
                v.tensor_scalar_mul(ln2f[:, 1024:1536], pp[2][:], 1.0 / D)
                return v.tensor_scalar_mul(ln2f[:, 1536:2048], pp[3][:], 1.0 / D)
            return fn
        ef1 = P.emit("v", mk_ef1(), waits=[P.wms("t", e_stat[DS - 1])], ms=True, tag="E.f1")

        def mk_ef2():
            def fn(v, nc_, S):
                v.tensor_tensor(ln2f[:, 2048:3072], ln2f[:, 0:1024], ln2f[:, 0:1024],
                                ALU.mult)
                return v.tensor_tensor(ln2f[:, 1024:2048], ln2f[:, 1024:2048],
                                       ln2f[:, 2048:3072], ALU.subtract)
            return fn
        ef2 = P.emit("v", mk_ef2(), waits=[P.wms("v", ef1)], ms=True, tag="E.f2")
        ef3 = P.emit("s", lambda s_, nc_, S: s_.activation(
            ln2f[:, 2048:3072], ln2f[:, 1024:2048], AF.Ln, bias=c_eps[:, 0:1]),
            waits=[P.wms("v", ef2)], ms=True, tag="E.f3")
        ef4 = P.emit("s", lambda s_, nc_, S: s_.activation(
            ln2b[:, 0:1024], ln2f[:, 2048:3072], AF.Exp, scale=-0.5),
            waits=[P.wms("s", ef3)], ms=True, tag="E.f4")
        ef5 = P.emit("v", lambda v, nc_, S: v.tensor_tensor(
            ln2b[:, 1024:2048], ln2f[:, 0:1024], ln2b[:, 0:1024], ALU.mult),
            waits=[P.wms("s", ef4)], ms=True, tag="E.f5")
        for k in range(DS):
            def mk_eap(k=k):
                def fn(v, nc_, S):
                    v.tensor_tensor(lnx2T[:, k * 1024:(k + 1) * 1024],
                                    x1bf[:, k * 1024:(k + 1) * 1024], ln2b[:, 0:1024],
                                    ALU.mult)
                    return v.tensor_tensor(lnx2T[:, k * 1024:(k + 1) * 1024],
                                           lnx2T[:, k * 1024:(k + 1) * 1024],
                                           ln2b[:, 1024:2048], ALU.subtract)
                return fn
            P.emit("v", mk_eap(), waits=[P.wms("v", ef5)], ms=True, tag=f"E.ap{k}")

        if upto == "E1":
            return finish([("lnx2T", lnx2T, DS * 1024), ("ln2f", ln2f, 3 * 1024),
                           ("ln2b", ln2b, 2 * 1024)], stacks=[g4, sE])
        P.barrier("postLN2")
        g4.close()

        g5 = ExitStack()
        e5 = g5.enter_context
        wgb = [e5(nc.sbuf_tensor(f"wgb{r}", [128, D], BF16)) for r in range(4)]
        wub = [e5(nc.sbuf_tensor(f"wub{r}", [128, D], BF16)) for r in range(4)]
        silb = [e5(nc.sbuf_tensor(f"silb{r}", [128, 1024], BF16)) for r in range(2)]
        upb = [e5(nc.sbuf_tensor(f"upb{r}", [128, 1024], BF16)) for r in range(2)]
        hb = [e5(nc.sbuf_tensor(f"hb{r}", [128, 1024], BF16)) for r in range(5)]

        wgu_ld = [None] * (2 * NI + 2)
        f_mm = [None] * NI; f_sil = [None] * NI; f_up = [None] * NI
        f_h = [None] * NI; f_hw = [None] * NI

        def emit_e2_load(it):
            if it >= NI + 1:
                return
            if it >= NI:
                wgu_ld[2 * it] = chase_g()
                wgu_ld[2 * it + 1] = chase_g()
                return
            r = it % 4
            w = [P.wms("t", f_mm[it - 4])] if it >= 4 else []
            wgu_ld[2 * it] = P.emit("g", (lambda it=it, r=r: lambda g, nc_, S:
                                    g.dma_start(wgb[r][:].rearrange("p (a c) -> p a c", a=DS),
                                                wgT_r[:, :, it * 128:(it + 1) * 128]))(),
                                    waits=w, dma=True, tag=f"E.wg{it}")
            wgu_ld[2 * it + 1] = P.emit("g", (lambda it=it, r=r: lambda g, nc_, S:
                                        g.dma_start(wub[r][:].rearrange("p (a c) -> p a c", a=DS),
                                                    wuT_r[:, :, it * 128:(it + 1) * 128]))(),
                                        dma=True, tag=f"E.wu{it}")

        emit_e2_load(0)
        emit_e2_load(1)
        emit_e2_load(2)

        for it in range(NI):
            emit_e2_load(it + 3)
            p = it % 2
            r = it % 4
            pg = (it % 2) * 4

            def mk_gmm(r=r, pg=pg):
                def fn(t, nc_, S):
                    last = None
                    for dh in range(DS):
                        lg = wgb[r][:, dh * 128:(dh + 1) * 128]
                        lu = wub[r][:, dh * 128:(dh + 1) * 128]
                        t.matmul(pp[pg][:], lg, lnx2T[:, dh * 1024:dh * 1024 + 512],
                                 start=(dh == 0), stop=(dh == DS - 1))
                        t.matmul(pp[pg + 1][:], lg,
                                 lnx2T[:, dh * 1024 + 512:(dh + 1) * 1024],
                                 start=(dh == 0), stop=(dh == DS - 1))
                        t.matmul(pp[pg + 2][:], lu, lnx2T[:, dh * 1024:dh * 1024 + 512],
                                 start=(dh == 0), stop=(dh == DS - 1))
                        last = t.matmul(pp[pg + 3][:], lu,
                                        lnx2T[:, dh * 1024 + 512:(dh + 1) * 1024],
                                        start=(dh == 0), stop=(dh == DS - 1))
                    return last
                return fn
            w = [P.wdma("g", wgu_ld[min(2 * it + 5, 2 * NI + 1)])]
            if it >= 2:
                w += [P.wms("s", f_sil[it - 2]), P.wms("v", f_up[it - 2])]
            f_mm[it] = P.emit("t", mk_gmm(), waits=w, ms=True, tag=f"E.mm{it}")

            def mk_sil(it=it, p=p, pg=pg):
                def fn(s_, nc_, S):
                    s_.activation(silb[p][:, 0:512], pp[pg][:], AF.Silu,
                                  scale=c_sg[:, it:it + 1])
                    return s_.activation(silb[p][:, 512:1024], pp[pg + 1][:], AF.Silu,
                                         scale=c_sg[:, it:it + 1])
                return fn
            w = [P.wms("t", f_mm[it])]
            if it >= 2:
                w.append(P.wms("v", f_h[it - 2]))
            f_sil[it] = P.emit("s", mk_sil(), waits=w, ms=True, tag=f"E.sl{it}")

            def mk_up(it=it, p=p, pg=pg):
                def fn(v, nc_, S):
                    v.tensor_scalar_mul(upb[p][:, 0:512], pp[pg + 2][:], c_su[:, it:it + 1])
                    return v.tensor_scalar_mul(upb[p][:, 512:1024], pp[pg + 3][:],
                                               c_su[:, it:it + 1])
                return fn
            f_up[it] = P.emit("v", mk_up(), waits=[P.wms("t", f_mm[it])], ms=True,
                              tag=f"E.up{it}")
            hr = it % 5

            def mk_h(p=p, hr=hr):
                def fn(v, nc_, S):
                    return v.tensor_tensor(hb[hr][:], silb[p][:], upb[p][:], ALU.mult)
                return fn
            w = [P.wms("s", f_sil[it]), P.wms("v", f_up[it])]
            if it >= 3:
                w.append(P.wdma("y", f_hw[it - 3]))
            f_h[it] = P.emit("v", mk_h(), waits=w, ms=True, tag=f"E.h{it}")
            f_hw[it] = P.emit("y", (lambda it=it, hr=hr: lambda y, nc_, S:
                              y.dma_start(h_r[:, it, :], hb[hr][:]))(),
                              waits=[P.wms("v", f_h[it])], dma=True, tag=f"E.hw{it}")

        if upto == "E2":
            return finish(stacks=[g5, sE])
        chase_y()
        chase_y()
        flush_y2 = P.dma["y"]
        P.emit("y", lambda y, nc_, S: y.wait_ge(sems["dma_y"], flush_y2), tag="E.flush")
        P.barrier("postE")
        g5.close()
        sE.close()

        g6 = ExitStack()
        e6 = g6.enter_context
        acc = e6(nc.sbuf_tensor("acc", [128, DS * 1024], F32))
        wdb = [e6(nc.sbuf_tensor(f"wdb{r}", [128, D], BF16)) for r in range(5)]
        hbf = [e6(nc.sbuf_tensor(f"hbf{r}", [128, 1024], BF16)) for r in range(8)]
        x1s = [e6(nc.sbuf_tensor(f"x1s{r}", [128, 1024], F32)) for r in range(4)]

        NBLK = (NI + F_BLK - 1) // F_BLK
        g_wd = [None] * (NI + 2)
        g_hl = [None] * (NI + 2)
        g_x1 = [None] * (DS + 2)
        g_mm = {}; g_dr = {}
        blk_mm_last = {}

        def emit_f_wd(it):
            if it >= NI:
                g_wd[it] = chase_g()
                return
            w = [P.wms("t", blk_mm_last[(it - 5) // F_BLK])] if it >= 5 else []
            g_wd[it] = P.emit("g", (lambda it=it: lambda g, nc_, S:
                              g.dma_start(wdb[it % 5][:], wdT_r[:, it, :]))(),
                              waits=w, dma=True, tag=f"F.wd{it}")

        def emit_f_h(it):
            if it >= NI:
                g_hl[it] = chase_y()
                return
            w = [P.wms("t", blk_mm_last[(it - 8) // F_BLK])] if it >= 8 else []
            g_hl[it] = P.emit("y", (lambda it=it: lambda y, nc_, S:
                              y.dma_start(hbf[it % 8][:], h_r[:, it, :]))(),
                              waits=w, dma=True, tag=f"F.h{it}")

        def emit_f_x1(dt):
            if dt >= DS + 2:
                return
            if dt >= DS:
                g_x1[dt] = chase_y()
                return
            w = [P.wms("v", g_dr[(0, dt - 4)])] if dt >= 4 else []
            g_x1[dt] = P.emit("y", (lambda dt=dt: lambda y, nc_, S:
                              y.dma_start(x1s[dt % 4][:], x1_r[:, dt, :]))(),
                              waits=w, dma=True, tag=f"F.x1{dt}")

        for it in range(5):
            emit_f_wd(it)
        for it in range(8):
            emit_f_h(it)
        for dt in range(3):
            emit_f_x1(dt)

        for blk in range(NBLK):
            its = list(range(blk * F_BLK, min((blk + 1) * F_BLK, NI)))
            nx0 = (blk + 1) * F_BLK
            if nx0 < NI + 2 and g_wd[nx0] is None:
                emit_f_wd(nx0)
            for it2 in range(nx0, min(nx0 + F_BLK, NI + 2)):
                if g_hl[it2] is None:
                    emit_f_h(it2)
            last_mm = None
            for dt in range(DS):
                pb = (dt % 3) * 2
                if blk == 0:
                    if g_x1[min(dt + 3, DS + 1)] is None:
                        emit_f_x1(dt + 3)

                def mk_fmm(dt=dt, its=tuple(its), pb=pb):
                    def fn(t, nc_, S):
                        last = None
                        for qi, it in enumerate(its):
                            lhsT = wdb[it % 5][:, dt * 128:(dt + 1) * 128]
                            t.matmul(pp[pb][:], lhsT, hbf[it % 8][:, 0:512],
                                     start=(qi == 0), stop=(qi == len(its) - 1))
                            last = t.matmul(pp[pb + 1][:], lhsT, hbf[it % 8][:, 512:1024],
                                            start=(qi == 0), stop=(qi == len(its) - 1))
                        return last
                    return fn
                wd_slack = g_wd[min(its[-1] + 1, NI + 1)]
                h_slack = g_hl[min(its[-1] + 2, NI + 1)]
                w = [P.wdma("g", wd_slack), P.wdma("y", h_slack)]
                if dt >= 3:
                    w.append(P.wms("v", g_dr[(blk, dt - 3)]))
                elif blk > 0:
                    w.append(P.wms("v", g_dr[(blk - 1, DS - 1)]))
                g_mm[(blk, dt)] = P.emit("t", mk_fmm(), waits=w, ms=True,
                                         tag=f"F.mm{blk}.{dt}")
                last_mm = g_mm[(blk, dt)]

                def mk_fdr(blk=blk, dt=dt, pb=pb):
                    def fn(v, nc_, S):
                        ac0 = acc[:, dt * 1024:dt * 1024 + 512]
                        ac1 = acc[:, dt * 1024 + 512:(dt + 1) * 1024]
                        if blk == 0:
                            v.scalar_tensor_tensor(ac0, pp[pb][:], c_sd[:, dt:dt + 1],
                                                   x1s[dt % 4][:, 0:512],
                                                   ALU.mult, ALU.add)
                            return v.scalar_tensor_tensor(ac1, pp[pb + 1][:],
                                                          c_sd[:, dt:dt + 1],
                                                          x1s[dt % 4][:, 512:1024],
                                                          ALU.mult, ALU.add)
                        v.scalar_tensor_tensor(ac0, pp[pb][:], c_sd[:, dt:dt + 1],
                                               ac0, ALU.mult, ALU.add)
                        return v.scalar_tensor_tensor(ac1, pp[pb + 1][:],
                                                      c_sd[:, dt:dt + 1],
                                                      ac1, ALU.mult, ALU.add)
                    return fn
                w = [P.wms("t", g_mm[(blk, dt)]), P.wdma("y", ld_sd)]
                if blk == 0:
                    w.append(P.wdma("y", g_x1[min(dt + 2, DS + 1)]))
                g_dr[(blk, dt)] = P.emit("v", mk_fdr(), waits=w, ms=True,
                                         tag=f"F.dr{blk}.{dt}")
            blk_mm_last[blk] = last_mm
            for it2 in range(nx0 + 1, min(nx0 + F_BLK, NI + 2)):
                if g_wd[it2] is None:
                    emit_f_wd(it2)

        for dt in range(DS):
            P.emit("y", (lambda dt=dt: lambda y, nc_, S:
                   y.dma_start(out_r[:, dt, :], acc[:, dt * 1024:(dt + 1) * 1024]))(),
                   waits=[P.wms("v", g_dr[(NBLK - 1, dt)])], dma=True, tag=f"OUT{dt}")
        final_dma_y = P.dma["y"]
        P.emit("y", lambda y, nc_, S: y.wait_ge(sems["dma_y"], final_dma_y), tag="final")
        g6.close()

        with nc.Block() as block:
            def mk_section(name):
                def sec(eng):
                    for ev in P.ev:
                        if ev.eng != name:
                            continue
                        for (kind, en, val) in ev.waits:
                            eng.wait_ge(sems[f"{'ms' if kind == 'ms' else 'dma'}_{en}"], val)
                        ins = ev.fn(eng, nc, sems)
                        for (kind, en, v) in ev.incs:
                            assert ins is not None, f"inc on None instr: {ev.tag}"
                            ins.then_inc(sems[f"{'ms' if kind == 'ms' else 'dma'}_{en}"], v)
                return sec
            block.sync(mk_section("y"))
            block.gpsimd(mk_section("g"))
            block.tensor(mk_section("t"))
            block.vector(mk_section("v"))
            block.scalar(mk_section("s"))

    return nc


_NC_CACHE = {}
_DEV_ECHO = False
_LAST_RES = None


def _get_nc():
    key = ("nc", _DEV_ECHO)
    if key not in _NC_CACHE:
        _NC_CACHE[key] = build_program(echo=_DEV_ECHO)
    return _NC_CACHE[key]


def kernel(L0_attn_x, L0_attn_wq_w, L0_attn_wq_b, L0_attn_wk_w, L0_attn_wk_b,
           L0_attn_wv_w, L0_attn_wv_b, L0_attn_wo_w, L0_attn_wo_b,
           L0_ffn_w_gate, L0_ffn_s_gate, L0_ffn_w_up, L0_ffn_s_up,
           L0_ffn_w_down, L0_ffn_s_down):
    import ml_dtypes
    x = np.asarray(L0_attn_x)
    xtb = [np.ascontiguousarray(x[b].T) for b in range(B)]
    wqTh = np.ascontiguousarray(np.asarray(L0_attn_wq_w).T)
    wkTh = np.ascontiguousarray(np.asarray(L0_attn_wk_w).T)
    wvTh = np.ascontiguousarray(np.asarray(L0_attn_wv_w).T)
    woTh = np.ascontiguousarray(np.asarray(L0_attn_wo_w).T)
    wgTh = np.ascontiguousarray(np.asarray(L0_ffn_w_gate).T)
    wuTh = np.ascontiguousarray(np.asarray(L0_ffn_w_up).T)
    wdTh = np.ascontiguousarray(np.asarray(L0_ffn_w_down).T)
    wqbh = np.ascontiguousarray(np.asarray(L0_attn_wq_b).reshape(H, 128).T)
    wkbh = np.ascontiguousarray(np.asarray(L0_attn_wk_b).reshape(128, 1))
    wvbh = np.ascontiguousarray(np.asarray(L0_attn_wv_b).reshape(1, 128))
    wobh = np.ascontiguousarray(np.asarray(L0_attn_wo_b).reshape(DS, 128).T)
    sgh = np.ascontiguousarray(np.asarray(L0_ffn_s_gate).reshape(NI, 128).T)
    suh = np.ascontiguousarray(np.asarray(L0_ffn_s_up).reshape(NI, 128).T)
    sdh = np.ascontiguousarray(np.asarray(L0_ffn_s_down).reshape(DS, 128).T)

    in_maps = []
    for c in range(N_CORES):
        own = _own_tiles(c)
        xo = np.ascontiguousarray(np.concatenate(
            [xtb[b][:, m * 128:(m + 1) * 128] for b in range(B) for m in own], axis=1))
        mk = np.zeros((128, NKS, 128), dtype=np.float32)
        for s in range(NKS):
            m = own[s // 8]
            if s < m:
                mk[:, s, :] = 1.0
            elif s == m:
                kabs = s * 128 + np.arange(128)[:, None]
                qabs = m * 128 + np.arange(128)[None, :]
                mk[:, s, :] = (kabs <= qabs).astype(np.float32)
        msk = np.ascontiguousarray(
            mk.reshape(128, NKS * 128).astype(ml_dtypes.bfloat16))
        in_maps.append(dict(
            xt0=xtb[0], xt1=xtb[1], xown=xo, mask01=msk,
            wqT=wqTh, wkT=wkTh, wvT=wvTh, woT=woTh,
            wqb=wqbh, wkb=wkbh, wvb=wvbh, wob=wobh,
            wgT=wgTh, wuT=wuTh, wdT=wdTh, sg=sgh, su=suh, sd=sdh,
        ))

    nc = _get_nc()
    res = run_bass_kernel_spmd(nc, in_maps, core_ids=list(range(N_CORES)))
    global _LAST_RES
    if _DEV_ECHO:
        _LAST_RES = (res, in_maps)

    outp = np.empty((B, T, D), dtype=np.float32)
    for c in range(N_CORES):
        oc = np.asarray(res.results[c]["out"])
        own = _own_tiles(c)
        for b in range(B):
            for j, m in enumerate(own):
                outp[b, m * 128:(m + 1) * 128, :] = \
                    oc[:, (b * 4 + j) * 128:(b * 4 + j + 1) * 128].T
    return outp


# version 04
# speedup vs baseline: 1.1350x; 1.1350x over previous
import numpy as np
import concourse.bass as bass
import concourse.mybir as mybir
from concourse.bass_utils import run_bass_kernel_spmd
from contextlib import ExitStack

F32 = mybir.dt.float32
BF16 = mybir.dt.bfloat16
I32 = mybir.dt.int32
AF = mybir.ActivationFunctionType
ALU = mybir.AluOpType
AX = mybir.AxisListType.X

D = 4096
T = 4096
B = 2
H = 32
HD = 128
INTER = 11008
NI = INTER // 128
DS = D // 128
NKS = T // 128
SCALE = 1.0 / float(np.sqrt(128.0))
N_CORES = 8
GRPS = [(3 * g, min(3 * g + 3, H)) for g in range((H + 2) // 3)]
F_BLK = 4


def _own_tiles(c):
    return [c, 15 - c, 16 + c, 31 - c]


class Ev:
    __slots__ = ("eng", "fn", "waits", "incs", "tag")

    def __init__(self, eng, fn, waits, incs, tag):
        self.eng, self.fn, self.waits, self.incs, self.tag = eng, fn, waits, incs, tag


class Prog:
    ENGS = ("y", "g", "t", "v", "s")

    def __init__(self):
        self.ev = []
        self.ms = {e: 0 for e in self.ENGS}
        self.dma = {e: 0 for e in self.ENGS}
        self._barr_n = 0

    def emit(self, eng, fn, waits=(), ms=False, dma=False, tag=""):
        incs = []
        mid = None
        if ms:
            self.ms[eng] += 1
            mid = self.ms[eng]
            incs.append(("ms", eng, 1))
        if dma:
            self.dma[eng] += 16
            mid = self.dma[eng]
            incs.append(("dma", eng, 16))
        self.ev.append(Ev(eng, fn, [w for w in waits if w is not None], incs, tag))
        return mid

    def wms(self, eng, mid):
        return ("ms", eng, mid) if mid is not None else None

    def wdma(self, eng, val):
        return ("dma", eng, val) if val is not None else None

    def barrier(self, tag="barr"):
        self._barr_n += 1
        n = self._barr_n

        def mk():
            def fn(eng, nc, sems):
                eng.drain()
                eng.sem_inc(sems["barr"], 1)
                return eng.wait_ge(sems["barr"], 5 * n)
            return fn
        for e in self.ENGS:
            self.ev.append(Ev(e, mk(), [], [], f"{tag}{n}"))



def build_program(upto=None, echo=False):
    nc = bass.Bass("TRN2", target_bir_lowering=False)
    P = Prog()

    xt = [nc.dram_tensor(f"xt{b}", [D, T], F32, kind="ExternalInput") for b in range(B)]
    xown = nc.dram_tensor("xown", [D, 1024], F32, kind="ExternalInput")
    mask01 = nc.dram_tensor("mask01", [128, NKS * 128], BF16, kind="ExternalInput")
    wqT = nc.dram_tensor("wqT", [D, D], F32, kind="ExternalInput")
    wkT = nc.dram_tensor("wkT", [D, HD], F32, kind="ExternalInput")
    wvT = nc.dram_tensor("wvT", [D, HD], F32, kind="ExternalInput")
    woT = nc.dram_tensor("woT", [D, D], F32, kind="ExternalInput")
    wqb = nc.dram_tensor("wqb", [128, H], F32, kind="ExternalInput")
    wkb = nc.dram_tensor("wkb", [128, 1], F32, kind="ExternalInput")
    wvb = nc.dram_tensor("wvb", [1, HD], F32, kind="ExternalInput")
    wob = nc.dram_tensor("wob", [128, DS], F32, kind="ExternalInput")
    wgT = nc.dram_tensor("wgT", [D, INTER], I32, kind="ExternalInput")
    wuT = nc.dram_tensor("wuT", [D, INTER], I32, kind="ExternalInput")
    wdT = nc.dram_tensor("wdT", [INTER, D], I32, kind="ExternalInput")
    sg = nc.dram_tensor("sg", [128, NI], F32, kind="ExternalInput")
    su = nc.dram_tensor("su", [128, NI], F32, kind="ExternalInput")
    sd = nc.dram_tensor("sd", [128, DS], F32, kind="ExternalInput")
    out = nc.dram_tensor("out", [D, 1024], F32, kind="ExternalOutput")
    scr_kind = "ExternalOutput" if upto else "Internal"
    x1_scr = nc.dram_tensor("x1_scr", [D, 1024], F32, kind=scr_kind)
    h_scr = nc.dram_tensor("h_scr", [INTER, 1024], BF16, kind=scr_kind)

    xt_r = [xt[b].rearrange("(dh p) t -> p dh t", p=128) for b in range(B)]
    xown_r = xown.rearrange("(dh p) t -> p dh t", p=128)
    wqT_r = wqT.rearrange("(dh p) o -> p dh o", p=128)
    wkT_r = wkT.rearrange("(dh p) o -> p dh o", p=128)
    wvT_r = wvT.rearrange("(dh p) o -> p dh o", p=128)
    woT_r = woT.rearrange("(dh p) o -> p dh o", p=128)
    wgT_r = wgT.rearrange("(dh p) i -> p dh i", p=128)
    wuT_r = wuT.rearrange("(dh p) i -> p dh i", p=128)
    wdT_r = wdT.rearrange("(ih p) o -> p ih o", p=128)
    x1_r = x1_scr.rearrange("(dh p) t -> p dh t", p=128)
    h_r = h_scr.rearrange("(ih p) t -> p ih t", p=128)
    out_r = out.rearrange("(dh p) t -> p dh t", p=128)

    ctx = ExitStack()
    e = ctx.enter_context

    def bc_mid(ap2d, reps):
        return bass.AP(ap2d.tensor, ap2d.offset, [ap2d.ap[0], [0, reps], ap2d.ap[1]])

    with ctx:
        c_ones = e(nc.sbuf_tensor("c_ones", [128, 128], BF16))
        c_eps = e(nc.sbuf_tensor("c_eps", [128, 1], F32))
        c_wqb = e(nc.sbuf_tensor("c_wqb", [128, H], F32))
        c_wkb = e(nc.sbuf_tensor("c_wkb", [128, 1], F32))
        c_wvbb = e(nc.sbuf_tensor("c_wvbb", [128, HD], BF16))
        c_wvbi = e(nc.sbuf_tensor("c_wvbi", [1, HD], BF16))
        c_wob = e(nc.sbuf_tensor("c_wob", [128, DS], F32))
        c_sg = e(nc.sbuf_tensor("c_sg", [128, NI], F32))
        c_su = e(nc.sbuf_tensor("c_su", [128, NI], F32))
        c_sd = e(nc.sbuf_tensor("c_sd", [128, DS], F32))
        c_km = e(nc.sbuf_tensor("c_km", [128, 16], F32))
        c_nb = e(nc.sbuf_tensor("c_nb", [128, 16], F32))
        c_chase = e(nc.sbuf_tensor("c_chase", [128, 64], F32))
        pp = [e(nc.psum_tensor(f"pp{i}", [128, 512], F32)) for i in range(8)]
        sems = {"barr": e(nc.semaphore("barr"))}
        for en in P.ENGS:
            sems[f"ms_{en}"] = e(nc.semaphore(f"ms_{en}"))
            sems[f"dma_{en}"] = e(nc.semaphore(f"dma_{en}"))

        def chase_g():
            return P.emit("g", lambda g, nc_, S:
                          g.dma_start(c_chase[:, 0:64], xt_r[0][:, 0, 0:64]),
                          dma=True, tag="chase.g")

        def chase_y():
            return P.emit("y", lambda y, nc_, S:
                          y.dma_start(c_chase[:, 0:64], xt_r[0][:, 0, 0:64]),
                          dma=True, tag="chase.y")

        def finish(dump_list=(), stacks=()):
            P.barrier("predump")
            for name, sb, ncols in dump_list:
                dbg = nc.dram_tensor("dbg_" + name, [128, ncols], sb.dtype,
                                     kind="ExternalOutput")
                nchunk = (ncols + 8191) // 8192
                for chv in range(nchunk):
                    lo = chv * 8192
                    hi = min(ncols, lo + 8192)
                    P.emit("y", (lambda sb=sb, dbg=dbg, lo=lo, hi=hi: lambda y, nc_, S:
                           y.dma_start(dbg[:, lo:hi], sb[:, lo:hi]))(), dma=True)
            fv = P.dma["y"]
            P.emit("y", lambda y, nc_, S: y.wait_ge(sems["dma_y"], fv), tag="dumpfinal")
            with nc.Block() as block:
                def mk_section(name):
                    def sec(eng):
                        for ev in P.ev:
                            if ev.eng != name:
                                continue
                            for (kind, en, val) in ev.waits:
                                eng.wait_ge(sems[f"{'ms' if kind == 'ms' else 'dma'}_{en}"], val)
                            ins = ev.fn(eng, nc, sems)
                            for (kind, en, v) in ev.incs:
                                assert ins is not None, f"inc on None instr: {ev.tag}"
                                ins.then_inc(sems[f"{'ms' if kind == 'ms' else 'dma'}_{en}"], v)
                    return sec
                block.sync(mk_section("y"))
                block.gpsimd(mk_section("g"))
                block.tensor(mk_section("t"))
                block.vector(mk_section("v"))
                block.scalar(mk_section("s"))
            for st_ in stacks:
                st_.close()
            return nc

        mid_ones = P.emit("g", lambda g, nc_, S: g.memset(c_ones[:], 1.0), ms=True)
        P.emit("g", lambda g, nc_, S: g.memset(c_eps[:], 1e-5), ms=True)
        ld_wqb = P.emit("y", lambda y, nc_, S: y.dma_start(c_wqb[:], wqb[:]), dma=True)
        P.emit("y", lambda y, nc_, S: y.dma_start(c_wkb[:], wkb[:]), dma=True)
        P.emit("y", lambda y, nc_, S: y.dma_start(c_wob[:], wob[:]), dma=True)
        P.emit("y", lambda y, nc_, S: y.dma_start(c_sg[:], sg[:]), dma=True)
        P.emit("y", lambda y, nc_, S: y.dma_start(c_su[:], su[:]), dma=True)
        ld_sd = P.emit("y", lambda y, nc_, S: y.dma_start(c_sd[:], sd[:]), dma=True)
        ld_vbi2 = P.emit("g", lambda g, nc_, S: g.dma_start(c_wvbi[:], wvb[:]), dma=True)

        if echo:
            e_xo = nc.dram_tensor("echo_xown", [D, 1024], F32, kind="ExternalOutput")
            e_xo_r = e_xo.rearrange("(a p) t -> p a t", p=128)
            e_buf = e(nc.sbuf_tensor("e_buf", [128, 1024], F32))
            prev = None
            for a in range(DS):
                w = [P.wdma("y", prev)] if prev is not None else []
                lid_ = P.emit("y", (lambda a=a: lambda y, nc_, S:
                              y.dma_start(e_buf[:], xown_r[:, a, :]))(),
                              waits=w, dma=True, tag=f"echo.l{a}")
                prev = P.emit("y", (lambda a=a: lambda y, nc_, S:
                              y.dma_start(e_xo_r[:, a, :], e_buf[:]))(),
                              waits=[P.wdma("y", lid_)], dma=True, tag=f"echo.s{a}")

        sAB = ExitStack()
        eAB = sAB.enter_context
        own_lnxT = eAB(nc.sbuf_tensor("own_lnxT", [128, DS * 1024], BF16))
        KT = eAB(nc.sbuf_tensor("KT", [128, B * T], BF16))
        VSB = eAB(nc.sbuf_tensor("VSB", [128, B * T], BF16))

        g1 = ExitStack()
        e1 = g1.enter_context
        xbf = [e1(nc.sbuf_tensor(f"xbf{p}", [128, D], BF16)) for p in range(4)]
        xsq = [e1(nc.sbuf_tensor(f"xsq{p}", [128, D], BF16)) for p in range(2)]
        chunk = [e1(nc.sbuf_tensor(f"chunk{p}", [128, D], BF16)) for p in range(3)]
        a_f32 = e1(nc.sbuf_tensor("a_f32", [128, 8 * 128], F32))
        a_bf = e1(nc.sbuf_tensor("a_bf", [128, 4 * 128], BF16))
        c_wkbf = e1(nc.sbuf_tensor("c_wkbf", [128, D], BF16))
        c_wvbf = e1(nc.sbuf_tensor("c_wvbf", [128, D], BF16))

        ld_wk = P.emit("g", lambda g, nc_, S: g.dma_start(
            c_wkbf[:].rearrange("p (a c) -> p a c", a=DS), wkT_r[:, :, :]), dma=True)
        ld_wv = P.emit("g", lambda g, nc_, S: g.dma_start(
            c_wvbf[:].rearrange("p (a c) -> p a c", a=DS), wvT_r[:, :, :]), dma=True)

        tiles = [("sh", b, s) for b in range(B) for s in range(NKS)] + \
                [("own", b, j) for b in range(B) for j in range(4)]
        NTL = len(tiles)
        a_ld = [None] * (NTL + 2)
        a_sq = [None] * NTL; a_st = [None] * NTL
        a_var = [None] * NTL; a_rstd = [None] * NTL; a_mr = [None] * NTL
        a_ap = [None] * NTL; a_kv = [None] * NTL; a_kd = [None] * NTL
        a_vd = [None] * NTL

        def emit_a_load(i):
            if i >= NTL + 2:
                return
            if i >= NTL:
                a_ld[i] = chase_g()
                return
            kind, b, s = tiles[i]
            p = i % 4
            if kind == "sh":
                srcap = xt_r[b][:, :, s * 128:(s + 1) * 128]
            else:
                srcap = xown_r[:, :, (b * 4 + s) * 128:(b * 4 + s + 1) * 128]
            w = []
            if i >= 4:
                w = [P.wms("s", a_sq[i - 4]), P.wms("t", a_st[i - 4]),
                     P.wms("v", a_ap[i - 4])]
            a_ld[i] = P.emit("g", (lambda srcap=srcap, p=p: lambda g, nc_, S:
                             g.dma_start(xbf[p][:].rearrange("p (a c) -> p a c", a=DS),
                                         srcap))(),
                             waits=w, dma=True, tag=f"A.ld{i}")

        emit_a_load(0)
        emit_a_load(1)
        emit_a_load(2)

        pending_kv = []

        def emit_kv(j):
            kindj, bj, sj = tiles[j]
            psqj = j % 2

            def mk_kv(psqj=psqj, j=j):
                def fn(t, nc_, S):
                    last = None
                    for dh in range(DS):
                        t.matmul(pp[2 + psqj][:, 0:128],
                                 c_wkbf[:, dh * 128:(dh + 1) * 128],
                                 chunk[j % 3][:, dh * 128:(dh + 1) * 128],
                                 start=(dh == 0), stop=False, skip_group_check=True)
                        last = t.matmul(pp[2 + psqj][:, 128:256],
                                        chunk[j % 3][:, dh * 128:(dh + 1) * 128],
                                        c_wvbf[:, dh * 128:(dh + 1) * 128],
                                        start=False, stop=(dh == DS - 1),
                                        skip_group_check=True)
                    return last
                return fn
            w = [P.wms("v", a_ap[j]), P.wdma("g", ld_wv)]
            prev = None
            for jj in range(j - 1, -1, -1):
                if tiles[jj][0] == "sh" and jj % 2 == psqj and a_vd[jj] is not None:
                    prev = jj
                    break
            if prev is not None:
                w.append(P.wms("v", a_vd[prev]))
            if j <= 1:
                w.append(P.wms("v", m_vbc))
            a_kv[j] = P.emit("t", mk_kv(), waits=w, ms=True, tag=f"A.kv{j}")
            koff = (bj * NKS + sj) * 128

            def mk_kd(psqj=psqj, koff=koff):
                def fn(v, nc_, S):
                    return v.tensor_scalar_add(KT[:, koff:koff + 128],
                                               pp[2 + psqj][:, 0:128], c_wkb[:, 0:1])
                return fn
            a_kd[j] = P.emit("v", mk_kd(), waits=[P.wms("t", a_kv[j])], ms=True,
                             tag=f"A.kd{j}")

            def mk_vd(psqj=psqj, koff=koff):
                def fn(v, nc_, S):
                    return v.tensor_tensor(VSB[:, koff:koff + 128],
                                           pp[2 + psqj][:, 128:256],
                                           c_wvbb[:], ALU.add)
                return fn
            a_vd[j] = P.emit("v", mk_vd(), waits=[P.wms("v", a_kd[j]),
                                                  P.wms("v", m_vbc)],
                             ms=True, tag=f"A.vd{j}")

        m_vb = P.emit("t", lambda t, nc_, S: t.matmul(pp[7][:, 0:HD], c_ones[0:1, :],
                                                      c_wvbi[:], start=True, stop=True),
                      waits=[P.wdma("g", ld_vbi2), P.wms("g", mid_ones),
                             P.wdma("g", a_ld[0])], ms=True)
        m_vbc = P.emit("v", lambda v, nc_, S: v.tensor_copy(c_wvbb[:], pp[7][:, 0:HD]),
                       waits=[P.wms("t", m_vb)], ms=True)

        for i, (kind, b, s) in enumerate(tiles):
            emit_a_load(i + 3)
            p = i % 4
            psq = i % 2
            slack = P.wdma("g", a_ld[min(i + 2, NTL + 1)])

            w = [slack]
            if i >= 2:
                w.append(P.wms("t", a_st[i - 2]))
            a_sq[i] = P.emit("s", (lambda p=p, psq=psq: lambda s_, nc_, S:
                             s_.activation(xsq[psq][:], xbf[p][:], AF.Square))(),
                             waits=w, ms=True, tag=f"A.sq{i}")

            def mk_stats(p=p, psq=psq):
                def fn(t, nc_, S):
                    last = None
                    for dh in range(DS):
                        t.matmul(pp[psq][:, 0:128], c_ones[:],
                                 xbf[p][:, dh * 128:(dh + 1) * 128],
                                 start=(dh == 0), stop=False, skip_group_check=True)
                        last = t.matmul(pp[psq][:, 128:256], c_ones[:],
                                        xsq[psq][:, dh * 128:(dh + 1) * 128],
                                        start=False, stop=(dh == DS - 1),
                                        skip_group_check=True)
                    return last
                return fn
            w = [slack, P.wms("s", a_sq[i]), P.wms("g", mid_ones)]
            if i >= 2:
                w.append(P.wms("v", a_var[i - 2]))
            a_st[i] = P.emit("t", mk_stats(), waits=w, ms=True, tag=f"A.st{i}")

            mo, eo, vo, to = psq * 512, psq * 512 + 128, psq * 512 + 256, psq * 512 + 384
            ro, mro = psq * 256, psq * 256 + 128

            def mk_me(psq=psq, mo=mo, eo=eo):
                def fn(v, nc_, S):
                    v.tensor_scalar_mul(a_f32[:, mo:mo + 128], pp[psq][:, 0:128], 1.0 / D)
                    return v.tensor_scalar_mul(a_f32[:, eo:eo + 128],
                                               pp[psq][:, 128:256], 1.0 / D)
                return fn
            w = [P.wms("t", a_st[i])]
            if i >= 2:
                w.append(P.wms("v", a_mr[i - 2]))
            me_id = P.emit("v", mk_me(), waits=w, ms=True, tag=f"A.me{i}")

            def mk_msq(mo=mo, to=to):
                def fn(v, nc_, S):
                    return v.tensor_tensor(a_f32[:, to:to + 128], a_f32[:, mo:mo + 128],
                                           a_f32[:, mo:mo + 128], ALU.mult)
                return fn
            msq_id = P.emit("v", mk_msq(), waits=[P.wms("v", me_id)], ms=True,
                            tag=f"A.msq{i}")

            def mk_var(eo=eo, vo=vo, to=to):
                def fn(v, nc_, S):
                    return v.tensor_tensor(a_f32[:, vo:vo + 128], a_f32[:, eo:eo + 128],
                                           a_f32[:, to:to + 128], ALU.subtract)
                return fn
            a_var[i] = P.emit("v", mk_var(), waits=[P.wms("v", msq_id)], ms=True,
                              tag=f"A.var{i}")

            def mk_ln(vo=vo, to=to):
                def fn(s_, nc_, S):
                    return s_.activation(a_f32[:, to:to + 128], a_f32[:, vo:vo + 128],
                                         AF.Ln, bias=c_eps[:, 0:1])
                return fn
            ln_id = P.emit("s", mk_ln(), waits=[P.wms("v", a_var[i])], ms=True,
                           tag=f"A.ln{i}")

            def mk_rstd(to=to, ro=ro):
                def fn(s_, nc_, S):
                    return s_.activation(a_bf[:, ro:ro + 128], a_f32[:, to:to + 128],
                                         AF.Exp, scale=-0.5)
                return fn
            a_rstd[i] = P.emit("s", mk_rstd(), waits=[P.wms("s", ln_id)], ms=True,
                               tag=f"A.rstd{i}")

            def mk_mr(mo=mo, ro=ro, mro=mro):
                def fn(v, nc_, S):
                    return v.tensor_tensor(a_bf[:, mro:mro + 128], a_f32[:, mo:mo + 128],
                                           a_bf[:, ro:ro + 128], ALU.mult)
                return fn
            a_mr[i] = P.emit("v", mk_mr(), waits=[P.wms("s", a_rstd[i])], ms=True,
                             tag=f"A.mr{i}")

            if kind == "own":
                dst3 = own_lnxT[:].rearrange("p (a c) -> p a c", c=1024)[
                    :, :, (b * 4 + s) * 128:(b * 4 + s + 1) * 128]
            else:
                dst3 = chunk[i % 3][:].rearrange("p (a c) -> p a c", a=DS)

            def mk_apply(p=p, dst3=dst3, ro=ro, mro=mro):
                def fn(v, nc_, S):
                    rb = bc_mid(a_bf[:, ro:ro + 128], DS)
                    mb = bc_mid(a_bf[:, mro:mro + 128], DS)
                    x3 = xbf[p][:].rearrange("p (a c) -> p a c", a=DS)
                    v.tensor_tensor(dst3, x3, rb, ALU.mult)
                    return v.tensor_tensor(dst3, dst3, mb, ALU.subtract)
                return fn
            w = [P.wms("v", a_mr[i]), slack]
            if kind == "sh" and i >= 3 and tiles[i - 3][0] == "sh":
                w.append(P.wms("t", a_kv[i - 3]))
            a_ap[i] = P.emit("v", mk_apply(), waits=w, ms=True, tag=f"A.ap{i}")

            if kind == "sh":
                pending_kv.append(i)
            while pending_kv and pending_kv[0] <= i - 2:
                emit_kv(pending_kv.pop(0))

        while pending_kv:
            emit_kv(pending_kv.pop(0))

        km_id = [None] * B
        prev_mm = None
        for b in range(B):
            w = [P.wms("v", a_kd[b * NKS + NKS - 1]), P.wms("t", a_st[NTL - 1]),
                 P.wms("t", a_kv[NTL - 9])]
            if prev_mm is not None:
                w.append(P.wms("t", prev_mm))
            ksq_id = P.emit("s", (lambda b=b: lambda s_, nc_, S:
                            s_.activation(xsq[0][:], KT[:, b * T:(b + 1) * T], AF.Square))(),
                            waits=w, ms=True, tag=f"K.sq{b}")
            red_ids = []
            for ch in range(8):
                w = [P.wms("s", ksq_id)]
                if ch == 0:
                    w.append(P.wms("v", a_vd[b * NKS + NKS - 1]))
                    w.append(P.wms("v", a_var[NTL - 1]))
                else:
                    w.append(P.wms("v", red_ids[-1]))
                mmid = P.emit("t", (lambda ch=ch: lambda t, nc_, S:
                              t.matmul(pp[0][:], c_ones[:],
                                       xsq[0][:, ch * 512:(ch + 1) * 512],
                                       start=True, stop=True))(),
                              waits=w, ms=True, tag=f"K.mm{b}.{ch}")
                red_ids.append(P.emit("v", (lambda ch=ch: lambda v, nc_, S:
                               v.reduce_max(c_km[:, 8 + ch:9 + ch], pp[0][:], axis=AX))(),
                               waits=[P.wms("t", mmid)], ms=True, tag=f"K.rd{b}.{ch}"))
                prev_mm = mmid
            km_id[b] = P.emit("v", (lambda b=b: lambda v, nc_, S:
                              v.reduce_max(c_km[:, b:b + 1], c_km[:, 8:16], axis=AX))(),
                              waits=[P.wms("v", red_ids[-1])], ms=True, tag=f"K.mx{b}")

        if upto == "A":
            return finish([("own_lnxT", own_lnxT, DS * 1024), ("KT", KT, B * T),
                           ("VSB", VSB, B * T), ("km", c_km, 16)], stacks=[g1, sAB])
        P.barrier("postA")
        g1.close()

        gBC = ExitStack()
        e2 = gBC.enter_context
        wqh = [e2(nc.sbuf_tensor(f"wqh{r}", [128, 2048], BF16)) for r in range(4)]
        QT = e2(nc.sbuf_tensor("QT", [128, 4 * 1024], BF16))
        expb = [e2(nc.sbuf_tensor(f"expb{r}", [128, 512], BF16)) for r in range(8)]
        recb = e2(nc.sbuf_tensor("recb", [128, 512], F32))
        c_mask = e2(nc.sbuf_tensor("c_mask", [128, NKS * 128], BF16))
        sCD = ExitStack()
        attnT = sCD.enter_context(nc.sbuf_tensor("attnT", [128, H * 1024], BF16,
                                                 side="right"))

        ld_msk = P.emit("y", lambda y, nc_, S: y.dma_start(c_mask[:], mask01[:]), dma=True)

        wq_ld = [None] * (2 * H + 2)
        qproj = {}
        qdrain = {}
        qsm_last = None
        scdone = {}
        cdrain = {}
        sc_list = []
        exp_list = []
        msk_list = []
        av_list = []

        def emit_wq_load(idx):
            if idx >= 2 * H + 2:
                return
            if idx >= 2 * H:
                wq_ld[idx] = chase_g()
                return
            h, half = divmod(idx, 2)
            ring = idx % 4
            w = []
            if idx >= 4:
                w.append(P.wms("t", qproj[(idx - 4) // 2]))
            srcap = wqT_r[:, 16 * half:16 * (half + 1), h * 128:(h + 1) * 128]
            wq_ld[idx] = P.emit(
                "g", (lambda ring=ring, srcap=srcap: lambda g, nc_, S:
                      g.dma_start(wqh[ring][:].rearrange("p (a c) -> p a c", a=16),
                                  srcap))(),
                waits=w, dma=True, tag=f"B.wq{h}.{half}")

        emit_wq_load(0)
        emit_wq_load(1)

        GRPS2 = [(2 * g, 2 * g + 2) for g in range(H // 2)]

        def emit_av(n, gi, b):
            s, hi = divmod(n, 2)
            jf = s // 8
            nq = (4 - jf) * 128
            qoff = jf * 128
            er = (gbase(gi, b) + n) % 8

            def mk_av(b=b, s=s, hi=hi, er=er, nq=nq, qoff=qoff):
                def fn(t, nc_, S):
                    t.matmul(pp[hi][:, qoff:qoff + nq],
                             VSB[:, (b * NKS + s) * 128:(b * NKS + s + 1) * 128],
                             expb[er][:, 0:nq],
                             start=(s == 0), stop=(s == NKS - 1),
                             skip_group_check=True)
                    return t.matmul(pp[2 + hi][:, qoff:qoff + nq],
                                    c_ones[:], expb[er][:, 0:nq],
                                    start=(s == 0), stop=(s == NKS - 1),
                                    skip_group_check=True)
                return fn
            w = [P.wms("v", msk_list[gbase(gi, b) + n])]
            if s == 0:
                pk = (gi, b - 1) if b == 1 else ((gi - 1, B - 1) if gi > 0 else None)
                if pk in cdrain:
                    w.append(P.wms("v", cdrain[pk]))
            av_id = P.emit("t", mk_av(), waits=w, ms=True, tag=f"C.av{gi}.{b}.{s}.{hi}")
            av_list.append(av_id)
            return av_id

        sweep_base = {}

        def gbase(gi, b):
            return sweep_base[(gi, b)]

        for gi, (h0, h1) in enumerate(GRPS2):
            nh = h1 - h0
            for h in range(h0, h1):
                emit_wq_load(2 * h + 2)
                emit_wq_load(2 * h + 3)
                pb = 4 + (h % 2) * 2

                def mk_qmm(h=h, pb=pb):
                    def fn(t, nc_, S):
                        last = None
                        for half in range(2):
                            ring = (h * 2 + half) % 4
                            for dhh in range(16):
                                dh = half * 16 + dhh
                                lhsT = wqh[ring][:, dhh * 128:(dhh + 1) * 128]
                                t.matmul(pp[pb][:], lhsT,
                                         own_lnxT[:, dh * 1024:dh * 1024 + 512],
                                         start=(dh == 0), stop=(dh == DS - 1))
                                last = t.matmul(pp[pb + 1][:], lhsT,
                                                own_lnxT[:, dh * 1024 + 512:(dh + 1) * 1024],
                                                start=(dh == 0), stop=(dh == DS - 1))
                        return last
                    return fn
                w = [P.wdma("g", wq_ld[min(2 * h + 3, 2 * H + 1)])]
                if h >= 2:
                    w.append(P.wms("v", qdrain[h - 2]))
                if gi > 0:
                    w.append(P.wms("v", cdrain[gi - 1]))
                    w.append(P.wms("s", exp_list[-1]))
                qproj[h] = P.emit("t", mk_qmm(), waits=w, ms=True, tag=f"B.qmm{h}")

                slot = h % 4

                def mk_qdr(h=h, pb=pb, slot=slot):
                    def fn(v, nc_, S):
                        v.tensor_scalar_add(QT[:, slot * 1024:slot * 1024 + 512],
                                            pp[pb][:], c_wqb[:, h:h + 1])
                        return v.tensor_scalar_add(QT[:, slot * 1024 + 512:(slot + 1) * 1024],
                                                   pp[pb + 1][:], c_wqb[:, h:h + 1])
                    return fn
                w = [P.wms("t", qproj[h]), P.wdma("y", ld_wqb)]
                if gi >= 1:
                    w.append(P.wms("t", scdone[gi - 1]))
                qdrain[h] = P.emit("v", mk_qdr(), waits=w, ms=True, tag=f"B.qdr{h}")

            last_red = None
            for hi, h in enumerate(range(h0, h1)):
                slot = h % 4

                def mk_qsq(slot=slot):
                    def fn(v, nc_, S):
                        v.tensor_tensor(expb[0][:], QT[:, slot * 1024:slot * 1024 + 512],
                                        QT[:, slot * 1024:slot * 1024 + 512], ALU.mult)
                        return v.tensor_tensor(expb[1][:],
                                               QT[:, slot * 1024 + 512:(slot + 1) * 1024],
                                               QT[:, slot * 1024 + 512:(slot + 1) * 1024],
                                               ALU.mult)
                    return fn
                w = [P.wms("v", qdrain[h])]
                if qsm_last is not None:
                    w.append(P.wms("t", qsm_last))
                if av_list:
                    w.append(P.wms("t", av_list[-1]))
                qsq_id = P.emit("v", mk_qsq(), waits=w, ms=True, tag=f"B.qsq{h}")

                def mk_qsm():
                    def fn(t, nc_, S):
                        t.matmul(pp[4][:], c_ones[:], expb[0][:], start=True, stop=True)
                        return t.matmul(pp[5][:], c_ones[:], expb[1][:],
                                        start=True, stop=True)
                    return fn
                w = [P.wms("v", qsq_id)]
                if gi > 0:
                    w.append(P.wms("v", cdrain[gi - 1]))
                if last_red is not None:
                    w.append(P.wms("v", last_red))
                qsm_last = P.emit("t", mk_qsm(), waits=w, ms=True, tag=f"B.qsm{h}")

                def mk_qrd(hi=hi):
                    def fn(v, nc_, S):
                        v.reduce_max(c_nb[:, 10 + 2 * hi:11 + 2 * hi], pp[4][:], axis=AX)
                        return v.reduce_max(c_nb[:, 11 + 2 * hi:12 + 2 * hi], pp[5][:],
                                            axis=AX)
                    return fn
                last_red = P.emit("v", mk_qrd(), waits=[P.wms("t", qsm_last)], ms=True,
                                  tag=f"B.qrd{h}")
            qmax_id = P.emit("v", (lambda nh=nh: lambda v, nc_, S:
                             v.reduce_max(c_nb[:, 0:1], c_nb[:, 10:10 + 2 * nh], axis=AX))(),
                             waits=[P.wms("v", last_red)], ms=True, tag=f"B.qmx{gi}")

            bc_ids = {}
            for b in range(B):
                base = 1 + b * 3
                pid = P.emit("v", (lambda b=b, base=base: lambda v, nc_, S:
                             v.tensor_tensor(c_nb[:, base:base + 1], c_nb[:, 0:1],
                                             c_km[:, b:b + 1], ALU.mult))(),
                             waits=[P.wms("v", qmax_id), P.wms("v", km_id[b])],
                             ms=True, tag=f"C.p{gi}.{b}")
                lid = P.emit("s", (lambda base=base: lambda s_, nc_, S:
                             s_.activation(c_nb[:, base + 1:base + 2],
                                           c_nb[:, base:base + 1],
                                           AF.Ln, bias=c_eps[:, 0:1]))(),
                             waits=[P.wms("v", pid)], ms=True, tag=f"C.l{gi}.{b}")
                eid = P.emit("s", (lambda base=base: lambda s_, nc_, S:
                             s_.activation(c_nb[:, base + 2:base + 3],
                                           c_nb[:, base + 1:base + 2],
                                           AF.Exp, scale=0.5))(),
                             waits=[P.wms("s", lid)], ms=True, tag=f"C.e{gi}.{b}")
                bc_ids[b] = P.emit("v", (lambda base=base, b=b: lambda v, nc_, S:
                                   v.tensor_scalar_mul(c_nb[:, 8 + b:9 + b],
                                                       c_nb[:, base + 2:base + 3],
                                                       -SCALE * 1.02))(),
                                   waits=[P.wms("s", eid)], ms=True, tag=f"C.bc{gi}.{b}")

            for b in range(B):
                sweep_base[(gi, b)] = len(sc_list)
                nb_steps = NKS * nh
                last_sc = None
                AV_DELAY = 3 * nh

                for s in range(NKS):
                    jf = s // 8
                    nq = (4 - jf) * 128
                    qoff = jf * 128
                    for hi, h in enumerate(range(h0, h1)):
                        slot = h % 4
                        n_loc = s * nh + hi
                        n = len(sc_list)
                        sb = 4 + (n % 4)
                        er = n % 8

                        def mk_sc(b=b, s=s, nq=nq, qoff=qoff, slot=slot, sb=sb):
                            def fn(t, nc_, S):
                                return t.matmul(
                                    pp[sb][:, 0:nq],
                                    KT[:, (b * NKS + s) * 128:(b * NKS + s + 1) * 128],
                                    QT[:, slot * 1024 + b * 512 + qoff:
                                       slot * 1024 + b * 512 + qoff + nq],
                                    start=True, stop=True, skip_group_check=True)
                            return fn
                        w = [P.wms("v", qdrain[h]), P.wms("v", bc_ids[b])]
                        if n >= 4:
                            w.append(P.wms("s", exp_list[n - 4]))
                        sc_id = P.emit("t", mk_sc(), waits=w, ms=True,
                                       tag=f"C.sc{gi}.{b}.{s}.{h}")
                        sc_list.append(sc_id)
                        last_sc = sc_id

                        def mk_exp(sb=sb, er=er, b=b, nq=nq):
                            def fn(s_, nc_, S):
                                return s_.activation(expb[er][:, 0:nq], pp[sb][:, 0:nq],
                                                     AF.Exp, bias=c_nb[:, 8 + b:9 + b],
                                                     scale=SCALE)
                            return fn
                        w = [P.wms("t", sc_id)]
                        if n >= 8:
                            w.append(P.wms("t", av_list[n - 8]))
                        exp_list.append(P.emit("s", mk_exp(), waits=w, ms=True,
                                               tag=f"C.exp{gi}.{b}.{s}.{h}"))

                        def mk_msk(er=er, s=s):
                            def fn(v, nc_, S):
                                return v.tensor_tensor(expb[er][:, 0:128],
                                                       expb[er][:, 0:128],
                                                       c_mask[:, s * 128:(s + 1) * 128],
                                                       ALU.mult)
                            return fn
                        msk_list.append(P.emit("v", mk_msk(),
                                        waits=[P.wms("s", exp_list[-1]),
                                               P.wdma("y", ld_msk)],
                                        ms=True, tag=f"C.mk{gi}.{b}.{s}.{h}"))

                        if n_loc >= AV_DELAY:
                            emit_av(n_loc - AV_DELAY, gi, b)
                for n_loc in range(nb_steps - AV_DELAY, nb_steps):
                    emit_av(n_loc, gi, b)

                dr_last = None
                for hi, h in enumerate(range(h0, h1)):
                    rec_id = P.emit("v", (lambda hi=hi: lambda v, nc_, S:
                                    v.reciprocal(recb[:], pp[2 + hi][:]))(),
                                    waits=[P.wms("t", av_list[-1])], ms=True,
                                    tag=f"C.rc{gi}.{b}.{h}")
                    aoff = h * 1024 + b * 512
                    dr_last = P.emit("v", (lambda hi=hi, aoff=aoff: lambda v, nc_, S:
                                     v.tensor_tensor(attnT[:, aoff:aoff + 512], pp[hi][:],
                                                     recb[:], ALU.mult))(),
                                     waits=[P.wms("v", rec_id)], ms=True,
                                     tag=f"C.dr{gi}.{b}.{h}")
                cdrain[(gi, b)] = dr_last
            scdone[gi] = last_sc
            cdrain[gi] = cdrain[(gi, B - 1)]

        if upto == "C":
            return finish([("attnT", attnT, H * 1024), ("QT", QT, 4 * 1024),
                           ("km", c_km, 16), ("nb", c_nb, 16)],
                          stacks=[gBC, sAB, sCD])
        P.barrier("postC")
        gBC.close()
        sAB.close()

        g3 = ExitStack()
        e3 = g3.enter_context
        wof = [e3(nc.sbuf_tensor(f"wof{r}", [128, D], BF16)) for r in range(4)]
        xow = [e3(nc.sbuf_tensor(f"xow{r}", [128, 1024], F32)) for r in range(4)]
        x1p = [e3(nc.sbuf_tensor(f"x1p{r}", [128, 1024], F32)) for r in range(4)]

        d_wo = [None] * (DS + 2); d_xo = [None] * (DS + 2)
        d_mm = [None] * DS; d_dr = [None] * DS; d_wr = [None] * DS

        def emit_d_wo(dt):
            if dt >= DS + 2:
                return
            if dt >= DS:
                d_wo[dt] = chase_g()
                return
            w = [P.wms("t", d_mm[dt - 4])] if dt >= 4 else []
            d_wo[dt] = P.emit("g", (lambda dt=dt: lambda g, nc_, S:
                              g.dma_start(wof[dt % 4][:].rearrange("p (a c) -> p a c", a=DS),
                                          woT_r[:, :, dt * 128:(dt + 1) * 128]))(),
                              waits=w, dma=True, tag=f"D.wo{dt}")

        def emit_d_xo(dt):
            if dt >= DS + 2:
                return
            if dt >= DS:
                d_xo[dt] = chase_y()
                return
            w = [P.wms("v", d_dr[dt - 4])] if dt >= 4 else []
            d_xo[dt] = P.emit("y", (lambda dt=dt: lambda y, nc_, S:
                              y.dma_start(xow[dt % 4][:], xown_r[:, dt, :]))(),
                              waits=w, dma=True, tag=f"D.xo{dt}")

        for dt in range(3):
            emit_d_wo(dt)
            emit_d_xo(dt)

        for dt in range(DS):
            emit_d_wo(dt + 3)
            emit_d_xo(dt + 3)
            pb = (dt % 3) * 2

            def mk_dmm(dt=dt, pb=pb):
                def fn(t, nc_, S):
                    last = None
                    for h in range(H):
                        lhsT = wof[dt % 4][:, h * 128:(h + 1) * 128]
                        t.matmul(pp[pb][:], lhsT, attnT[:, h * 1024:h * 1024 + 512],
                                 start=(h == 0), stop=(h == H - 1))
                        last = t.matmul(pp[pb + 1][:], lhsT,
                                        attnT[:, h * 1024 + 512:(h + 1) * 1024],
                                        start=(h == 0), stop=(h == H - 1))
                    return last
                return fn
            w = [P.wdma("g", d_wo[min(dt + 2, DS + 1)])]
            if dt >= 3:
                w.append(P.wms("v", d_dr[dt - 3]))
            d_mm[dt] = P.emit("t", mk_dmm(), waits=w, ms=True, tag=f"D.mm{dt}")

            def mk_ddr(dt=dt, pb=pb):
                def fn(v, nc_, S):
                    v.scalar_tensor_tensor(x1p[dt % 4][:, 0:512], pp[pb][:],
                                           c_wob[:, dt:dt + 1],
                                           xow[dt % 4][:, 0:512], ALU.add, ALU.add)
                    return v.scalar_tensor_tensor(x1p[dt % 4][:, 512:1024], pp[pb + 1][:],
                                                  c_wob[:, dt:dt + 1],
                                                  xow[dt % 4][:, 512:1024],
                                                  ALU.add, ALU.add)
                return fn
            w = [P.wms("t", d_mm[dt]), P.wdma("y", d_xo[min(dt + 2, DS + 1)])]
            if dt >= 2:
                w.append(P.wdma("y", d_wr[dt - 2]))
            d_dr[dt] = P.emit("v", mk_ddr(), waits=w, ms=True, tag=f"D.dr{dt}")
            d_wr[dt] = P.emit("y", (lambda dt=dt: lambda y, nc_, S:
                              y.dma_start(x1_r[:, dt, :], x1p[dt % 4][:]))(),
                              waits=[P.wms("v", d_dr[dt])], dma=True, tag=f"D.wr{dt}")

        if upto == "D":
            return finish(stacks=[g3, sCD])
        chase_y()
        chase_y()
        flush_y = P.dma["y"]
        P.emit("y", lambda y, nc_, S: y.wait_ge(sems["dma_y"], flush_y), tag="D.flush")
        P.barrier("postD")
        g3.close()
        sCD.close()

        sE = ExitStack()
        lnx2T = sE.enter_context(nc.sbuf_tensor("lnx2T", [128, DS * 1024], BF16,
                                                side="right"))
        g4 = ExitStack()
        e4 = g4.enter_context
        x1bf = e4(nc.sbuf_tensor("x1bf", [128, DS * 1024], BF16))
        x1sq = [e4(nc.sbuf_tensor(f"x1sq{r}", [128, 1024], BF16)) for r in range(2)]
        ln2f = e4(nc.sbuf_tensor("ln2f", [128, 3 * 1024], F32))
        ln2b = e4(nc.sbuf_tensor("ln2b", [128, 2 * 1024], BF16))

        e_ld = [None] * (DS + 2)
        for k in range(DS):
            e_ld[k] = P.emit("g", (lambda k=k: lambda g, nc_, S:
                             g.dma_start(x1bf[:, k * 1024:(k + 1) * 1024], x1_r[:, k, :]))(),
                             dma=True, tag=f"E.ld{k}")
        e_ld[DS] = chase_g()
        e_ld[DS + 1] = chase_g()

        e_sq = [None] * DS; e_stat = [None] * DS
        for k in range(DS):
            p = k % 2
            slack = P.wdma("g", e_ld[min(k + 2, DS + 1)])
            w = [slack]
            if k >= 2:
                w.append(P.wms("t", e_stat[k - 2]))
            e_sq[k] = P.emit("s", (lambda k=k, p=p: lambda s_, nc_, S:
                             s_.activation(x1sq[p][:], x1bf[:, k * 1024:(k + 1) * 1024],
                                           AF.Square))(),
                             waits=w, ms=True, tag=f"E.sq{k}")

            def mk_est(k=k, p=p):
                def fn(t, nc_, S):
                    t.matmul(pp[0][:], c_ones[:], x1bf[:, k * 1024:k * 1024 + 512],
                             start=(k == 0), stop=(k == DS - 1))
                    t.matmul(pp[1][:], c_ones[:], x1bf[:, k * 1024 + 512:(k + 1) * 1024],
                             start=(k == 0), stop=(k == DS - 1))
                    t.matmul(pp[2][:], c_ones[:], x1sq[p][:, 0:512],
                             start=(k == 0), stop=(k == DS - 1))
                    return t.matmul(pp[3][:], c_ones[:], x1sq[p][:, 512:1024],
                                    start=(k == 0), stop=(k == DS - 1))
                return fn
            e_stat[k] = P.emit("t", mk_est(), waits=[slack, P.wms("s", e_sq[k])],
                               ms=True, tag=f"E.st{k}")

        def mk_ef1():
            def fn(v, nc_, S):
                v.tensor_scalar_mul(ln2f[:, 0:512], pp[0][:], 1.0 / D)
                v.tensor_scalar_mul(ln2f[:, 512:1024], pp[1][:], 1.0 / D)
                v.tensor_scalar_mul(ln2f[:, 1024:1536], pp[2][:], 1.0 / D)
                return v.tensor_scalar_mul(ln2f[:, 1536:2048], pp[3][:], 1.0 / D)
            return fn
        ef1 = P.emit("v", mk_ef1(), waits=[P.wms("t", e_stat[DS - 1])], ms=True, tag="E.f1")

        def mk_ef2():
            def fn(v, nc_, S):
                v.tensor_tensor(ln2f[:, 2048:3072], ln2f[:, 0:1024], ln2f[:, 0:1024],
                                ALU.mult)
                return v.tensor_tensor(ln2f[:, 1024:2048], ln2f[:, 1024:2048],
                                       ln2f[:, 2048:3072], ALU.subtract)
            return fn
        ef2 = P.emit("v", mk_ef2(), waits=[P.wms("v", ef1)], ms=True, tag="E.f2")
        ef3 = P.emit("s", lambda s_, nc_, S: s_.activation(
            ln2f[:, 2048:3072], ln2f[:, 1024:2048], AF.Ln, bias=c_eps[:, 0:1]),
            waits=[P.wms("v", ef2)], ms=True, tag="E.f3")
        ef4 = P.emit("s", lambda s_, nc_, S: s_.activation(
            ln2b[:, 0:1024], ln2f[:, 2048:3072], AF.Exp, scale=-0.5),
            waits=[P.wms("s", ef3)], ms=True, tag="E.f4")
        ef5 = P.emit("v", lambda v, nc_, S: v.tensor_tensor(
            ln2b[:, 1024:2048], ln2f[:, 0:1024], ln2b[:, 0:1024], ALU.mult),
            waits=[P.wms("s", ef4)], ms=True, tag="E.f5")
        for k in range(DS):
            def mk_eap(k=k):
                def fn(v, nc_, S):
                    v.tensor_tensor(lnx2T[:, k * 1024:(k + 1) * 1024],
                                    x1bf[:, k * 1024:(k + 1) * 1024], ln2b[:, 0:1024],
                                    ALU.mult)
                    return v.tensor_tensor(lnx2T[:, k * 1024:(k + 1) * 1024],
                                           lnx2T[:, k * 1024:(k + 1) * 1024],
                                           ln2b[:, 1024:2048], ALU.subtract)
                return fn
            P.emit("v", mk_eap(), waits=[P.wms("v", ef5)], ms=True, tag=f"E.ap{k}")

        if upto == "E1":
            return finish([("lnx2T", lnx2T, DS * 1024), ("ln2f", ln2f, 3 * 1024),
                           ("ln2b", ln2b, 2 * 1024)], stacks=[g4, sE])
        P.barrier("postLN2")
        g4.close()

        g5 = ExitStack()
        e5 = g5.enter_context
        wgb = [e5(nc.sbuf_tensor(f"wgb{r}", [128, D], BF16)) for r in range(4)]
        wub = [e5(nc.sbuf_tensor(f"wub{r}", [128, D], BF16)) for r in range(4)]
        silb = [e5(nc.sbuf_tensor(f"silb{r}", [128, 1024], BF16)) for r in range(2)]
        upb = [e5(nc.sbuf_tensor(f"upb{r}", [128, 1024], BF16)) for r in range(2)]
        hb = [e5(nc.sbuf_tensor(f"hb{r}", [128, 1024], BF16)) for r in range(5)]

        wgu_ld = [None] * (2 * NI + 2)
        f_mm = [None] * NI; f_sil = [None] * NI; f_up = [None] * NI
        f_h = [None] * NI; f_hw = [None] * NI

        def emit_e2_load(it):
            if it >= NI + 1:
                return
            if it >= NI:
                wgu_ld[2 * it] = chase_g()
                wgu_ld[2 * it + 1] = chase_g()
                return
            r = it % 4
            w = [P.wms("t", f_mm[it - 4])] if it >= 4 else []
            wgu_ld[2 * it] = P.emit("g", (lambda it=it, r=r: lambda g, nc_, S:
                                    g.dma_start(wgb[r][:].rearrange("p (a c) -> p a c", a=DS),
                                                wgT_r[:, :, it * 128:(it + 1) * 128]))(),
                                    waits=w, dma=True, tag=f"E.wg{it}")
            wgu_ld[2 * it + 1] = P.emit("g", (lambda it=it, r=r: lambda g, nc_, S:
                                        g.dma_start(wub[r][:].rearrange("p (a c) -> p a c", a=DS),
                                                    wuT_r[:, :, it * 128:(it + 1) * 128]))(),
                                        dma=True, tag=f"E.wu{it}")

        emit_e2_load(0)
        emit_e2_load(1)
        emit_e2_load(2)

        for it in range(NI):
            emit_e2_load(it + 3)
            p = it % 2
            r = it % 4
            pg = (it % 2) * 4

            def mk_gmm(r=r, pg=pg):
                def fn(t, nc_, S):
                    last = None
                    for dh in range(DS):
                        lg = wgb[r][:, dh * 128:(dh + 1) * 128]
                        lu = wub[r][:, dh * 128:(dh + 1) * 128]
                        t.matmul(pp[pg][:], lg, lnx2T[:, dh * 1024:dh * 1024 + 512],
                                 start=(dh == 0), stop=(dh == DS - 1))
                        t.matmul(pp[pg + 1][:], lg,
                                 lnx2T[:, dh * 1024 + 512:(dh + 1) * 1024],
                                 start=(dh == 0), stop=(dh == DS - 1))
                        t.matmul(pp[pg + 2][:], lu, lnx2T[:, dh * 1024:dh * 1024 + 512],
                                 start=(dh == 0), stop=(dh == DS - 1))
                        last = t.matmul(pp[pg + 3][:], lu,
                                        lnx2T[:, dh * 1024 + 512:(dh + 1) * 1024],
                                        start=(dh == 0), stop=(dh == DS - 1))
                    return last
                return fn
            w = [P.wdma("g", wgu_ld[min(2 * it + 5, 2 * NI + 1)])]
            if it >= 2:
                w += [P.wms("s", f_sil[it - 2]), P.wms("v", f_up[it - 2])]
            f_mm[it] = P.emit("t", mk_gmm(), waits=w, ms=True, tag=f"E.mm{it}")

            def mk_sil(it=it, p=p, pg=pg):
                def fn(s_, nc_, S):
                    s_.activation(silb[p][:, 0:512], pp[pg][:], AF.Silu,
                                  scale=c_sg[:, it:it + 1])
                    return s_.activation(silb[p][:, 512:1024], pp[pg + 1][:], AF.Silu,
                                         scale=c_sg[:, it:it + 1])
                return fn
            w = [P.wms("t", f_mm[it])]
            if it >= 2:
                w.append(P.wms("v", f_h[it - 2]))
            f_sil[it] = P.emit("s", mk_sil(), waits=w, ms=True, tag=f"E.sl{it}")

            def mk_up(it=it, p=p, pg=pg):
                def fn(v, nc_, S):
                    v.tensor_scalar_mul(upb[p][:, 0:512], pp[pg + 2][:], c_su[:, it:it + 1])
                    return v.tensor_scalar_mul(upb[p][:, 512:1024], pp[pg + 3][:],
                                               c_su[:, it:it + 1])
                return fn
            f_up[it] = P.emit("v", mk_up(), waits=[P.wms("t", f_mm[it])], ms=True,
                              tag=f"E.up{it}")
            hr = it % 5

            def mk_h(p=p, hr=hr):
                def fn(v, nc_, S):
                    return v.tensor_tensor(hb[hr][:], silb[p][:], upb[p][:], ALU.mult)
                return fn
            w = [P.wms("s", f_sil[it]), P.wms("v", f_up[it])]
            if it >= 3:
                w.append(P.wdma("y", f_hw[it - 3]))
            f_h[it] = P.emit("v", mk_h(), waits=w, ms=True, tag=f"E.h{it}")
            f_hw[it] = P.emit("y", (lambda it=it, hr=hr: lambda y, nc_, S:
                              y.dma_start(h_r[:, it, :], hb[hr][:]))(),
                              waits=[P.wms("v", f_h[it])], dma=True, tag=f"E.hw{it}")

        if upto == "E2":
            return finish(stacks=[g5, sE])
        chase_y()
        chase_y()
        flush_y2 = P.dma["y"]
        P.emit("y", lambda y, nc_, S: y.wait_ge(sems["dma_y"], flush_y2), tag="E.flush")
        P.barrier("postE")
        g5.close()
        sE.close()

        g6 = ExitStack()
        e6 = g6.enter_context
        acc = e6(nc.sbuf_tensor("acc", [128, DS * 1024], F32))
        wdb = [e6(nc.sbuf_tensor(f"wdb{r}", [128, D], BF16)) for r in range(5)]
        hbf = [e6(nc.sbuf_tensor(f"hbf{r}", [128, 1024], BF16)) for r in range(8)]
        x1s = [e6(nc.sbuf_tensor(f"x1s{r}", [128, 1024], F32)) for r in range(4)]

        NBLK = (NI + F_BLK - 1) // F_BLK
        g_wd = [None] * (NI + 2)
        g_hl = [None] * (NI + 2)
        g_x1 = [None] * (DS + 2)
        g_mm = {}; g_dr = {}
        blk_mm_last = {}

        def emit_f_wd(it):
            if it >= NI:
                g_wd[it] = chase_g()
                return
            w = [P.wms("t", blk_mm_last[(it - 5) // F_BLK])] if it >= 5 else []
            g_wd[it] = P.emit("g", (lambda it=it: lambda g, nc_, S:
                              g.dma_start(wdb[it % 5][:], wdT_r[:, it, :]))(),
                              waits=w, dma=True, tag=f"F.wd{it}")

        def emit_f_h(it):
            if it >= NI:
                g_hl[it] = chase_y()
                return
            w = [P.wms("t", blk_mm_last[(it - 8) // F_BLK])] if it >= 8 else []
            g_hl[it] = P.emit("y", (lambda it=it: lambda y, nc_, S:
                              y.dma_start(hbf[it % 8][:], h_r[:, it, :]))(),
                              waits=w, dma=True, tag=f"F.h{it}")

        def emit_f_x1(dt):
            if dt >= DS + 2:
                return
            if dt >= DS:
                g_x1[dt] = chase_y()
                return
            w = [P.wms("v", g_dr[(0, dt - 4)])] if dt >= 4 else []
            g_x1[dt] = P.emit("y", (lambda dt=dt: lambda y, nc_, S:
                              y.dma_start(x1s[dt % 4][:], x1_r[:, dt, :]))(),
                              waits=w, dma=True, tag=f"F.x1{dt}")

        for it in range(5):
            emit_f_wd(it)
        for it in range(8):
            emit_f_h(it)
        for dt in range(3):
            emit_f_x1(dt)

        for blk in range(NBLK):
            its = list(range(blk * F_BLK, min((blk + 1) * F_BLK, NI)))
            nx0 = (blk + 1) * F_BLK
            if nx0 < NI + 2 and g_wd[nx0] is None:
                emit_f_wd(nx0)
            for it2 in range(nx0, min(nx0 + F_BLK, NI + 2)):
                if g_hl[it2] is None:
                    emit_f_h(it2)
            last_mm = None
            for dt in range(DS):
                pb = (dt % 3) * 2
                if blk == 0:
                    if g_x1[min(dt + 3, DS + 1)] is None:
                        emit_f_x1(dt + 3)

                def mk_fmm(dt=dt, its=tuple(its), pb=pb):
                    def fn(t, nc_, S):
                        last = None
                        for qi, it in enumerate(its):
                            lhsT = wdb[it % 5][:, dt * 128:(dt + 1) * 128]
                            t.matmul(pp[pb][:], lhsT, hbf[it % 8][:, 0:512],
                                     start=(qi == 0), stop=(qi == len(its) - 1))
                            last = t.matmul(pp[pb + 1][:], lhsT, hbf[it % 8][:, 512:1024],
                                            start=(qi == 0), stop=(qi == len(its) - 1))
                        return last
                    return fn
                wd_slack = g_wd[min(its[-1] + 1, NI + 1)]
                h_slack = g_hl[min(its[-1] + 2, NI + 1)]
                w = [P.wdma("g", wd_slack), P.wdma("y", h_slack)]
                if dt >= 3:
                    w.append(P.wms("v", g_dr[(blk, dt - 3)]))
                elif blk > 0:
                    w.append(P.wms("v", g_dr[(blk - 1, DS - 1)]))
                g_mm[(blk, dt)] = P.emit("t", mk_fmm(), waits=w, ms=True,
                                         tag=f"F.mm{blk}.{dt}")
                last_mm = g_mm[(blk, dt)]

                def mk_fdr(blk=blk, dt=dt, pb=pb):
                    def fn(v, nc_, S):
                        ac0 = acc[:, dt * 1024:dt * 1024 + 512]
                        ac1 = acc[:, dt * 1024 + 512:(dt + 1) * 1024]
                        if blk == 0:
                            v.scalar_tensor_tensor(ac0, pp[pb][:], c_sd[:, dt:dt + 1],
                                                   x1s[dt % 4][:, 0:512],
                                                   ALU.mult, ALU.add)
                            return v.scalar_tensor_tensor(ac1, pp[pb + 1][:],
                                                          c_sd[:, dt:dt + 1],
                                                          x1s[dt % 4][:, 512:1024],
                                                          ALU.mult, ALU.add)
                        v.scalar_tensor_tensor(ac0, pp[pb][:], c_sd[:, dt:dt + 1],
                                               ac0, ALU.mult, ALU.add)
                        return v.scalar_tensor_tensor(ac1, pp[pb + 1][:],
                                                      c_sd[:, dt:dt + 1],
                                                      ac1, ALU.mult, ALU.add)
                    return fn
                w = [P.wms("t", g_mm[(blk, dt)]), P.wdma("y", ld_sd)]
                if blk == 0:
                    w.append(P.wdma("y", g_x1[min(dt + 2, DS + 1)]))
                g_dr[(blk, dt)] = P.emit("v", mk_fdr(), waits=w, ms=True,
                                         tag=f"F.dr{blk}.{dt}")
            blk_mm_last[blk] = last_mm
            for it2 in range(nx0 + 1, min(nx0 + F_BLK, NI + 2)):
                if g_wd[it2] is None:
                    emit_f_wd(it2)

        for dt in range(DS):
            P.emit("y", (lambda dt=dt: lambda y, nc_, S:
                   y.dma_start(out_r[:, dt, :], acc[:, dt * 1024:(dt + 1) * 1024]))(),
                   waits=[P.wms("v", g_dr[(NBLK - 1, dt)])], dma=True, tag=f"OUT{dt}")
        final_dma_y = P.dma["y"]
        P.emit("y", lambda y, nc_, S: y.wait_ge(sems["dma_y"], final_dma_y), tag="final")
        g6.close()

        with nc.Block() as block:
            def mk_section(name):
                def sec(eng):
                    for ev in P.ev:
                        if ev.eng != name:
                            continue
                        for (kind, en, val) in ev.waits:
                            eng.wait_ge(sems[f"{'ms' if kind == 'ms' else 'dma'}_{en}"], val)
                        ins = ev.fn(eng, nc, sems)
                        for (kind, en, v) in ev.incs:
                            assert ins is not None, f"inc on None instr: {ev.tag}"
                            ins.then_inc(sems[f"{'ms' if kind == 'ms' else 'dma'}_{en}"], v)
                return sec
            block.sync(mk_section("y"))
            block.gpsimd(mk_section("g"))
            block.tensor(mk_section("t"))
            block.vector(mk_section("v"))
            block.scalar(mk_section("s"))

    return nc


_NC_CACHE = {}
_DEV_ECHO = False
_LAST_RES = None


def _get_nc():
    key = ("nc", _DEV_ECHO)
    if key not in _NC_CACHE:
        _NC_CACHE[key] = build_program(echo=_DEV_ECHO)
    return _NC_CACHE[key]


def kernel(L0_attn_x, L0_attn_wq_w, L0_attn_wq_b, L0_attn_wk_w, L0_attn_wk_b,
           L0_attn_wv_w, L0_attn_wv_b, L0_attn_wo_w, L0_attn_wo_b,
           L0_ffn_w_gate, L0_ffn_s_gate, L0_ffn_w_up, L0_ffn_s_up,
           L0_ffn_w_down, L0_ffn_s_down):
    import ml_dtypes
    x = np.asarray(L0_attn_x)
    xtb = [np.ascontiguousarray(x[b].T) for b in range(B)]
    wqTh = np.ascontiguousarray(np.asarray(L0_attn_wq_w).T)
    wkTh = np.ascontiguousarray(np.asarray(L0_attn_wk_w).T)
    wvTh = np.ascontiguousarray(np.asarray(L0_attn_wv_w).T)
    woTh = np.ascontiguousarray(np.asarray(L0_attn_wo_w).T)
    wgTh = np.ascontiguousarray(np.asarray(L0_ffn_w_gate).T)
    wuTh = np.ascontiguousarray(np.asarray(L0_ffn_w_up).T)
    wdTh = np.ascontiguousarray(np.asarray(L0_ffn_w_down).T)
    wqbh = np.ascontiguousarray(np.asarray(L0_attn_wq_b).reshape(H, 128).T)
    wkbh = np.ascontiguousarray(np.asarray(L0_attn_wk_b).reshape(128, 1))
    wvbh = np.ascontiguousarray(np.asarray(L0_attn_wv_b).reshape(1, 128))
    wobh = np.ascontiguousarray(np.asarray(L0_attn_wo_b).reshape(DS, 128).T)
    sgh = np.ascontiguousarray(np.asarray(L0_ffn_s_gate).reshape(NI, 128).T)
    suh = np.ascontiguousarray(np.asarray(L0_ffn_s_up).reshape(NI, 128).T)
    sdh = np.ascontiguousarray(np.asarray(L0_ffn_s_down).reshape(DS, 128).T)

    in_maps = []
    for c in range(N_CORES):
        own = _own_tiles(c)
        xo = np.ascontiguousarray(np.concatenate(
            [xtb[b][:, m * 128:(m + 1) * 128] for b in range(B) for m in own], axis=1))
        mk = np.zeros((128, NKS, 128), dtype=np.float32)
        for s in range(NKS):
            m = own[s // 8]
            if s < m:
                mk[:, s, :] = 1.0
            elif s == m:
                kabs = s * 128 + np.arange(128)[:, None]
                qabs = m * 128 + np.arange(128)[None, :]
                mk[:, s, :] = (kabs <= qabs).astype(np.float32)
        msk = np.ascontiguousarray(
            mk.reshape(128, NKS * 128).astype(ml_dtypes.bfloat16))
        in_maps.append(dict(
            xt0=xtb[0], xt1=xtb[1], xown=xo, mask01=msk,
            wqT=wqTh, wkT=wkTh, wvT=wvTh, woT=woTh,
            wqb=wqbh, wkb=wkbh, wvb=wvbh, wob=wobh,
            wgT=wgTh, wuT=wuTh, wdT=wdTh, sg=sgh, su=suh, sd=sdh,
        ))

    nc = _get_nc()
    res = run_bass_kernel_spmd(nc, in_maps, core_ids=list(range(N_CORES)))
    global _LAST_RES
    if _DEV_ECHO:
        _LAST_RES = (res, in_maps)

    outp = np.empty((B, T, D), dtype=np.float32)
    for c in range(N_CORES):
        oc = np.asarray(res.results[c]["out"])
        own = _own_tiles(c)
        for b in range(B):
            for j, m in enumerate(own):
                outp[b, m * 128:(m + 1) * 128, :] = \
                    oc[:, (b * 4 + j) * 128:(b * 4 + j + 1) * 128].T
    return outp
